# version 15
# baseline (speedup 1.0000x reference)
import numpy as np
import ml_dtypes
import concourse.bass as bass
import concourse.mybir as mybir
from concourse.bass_utils import run_bass_kernel_spmd

F32 = mybir.dt.float32
BF16 = mybir.dt.bfloat16
AF = mybir.ActivationFunctionType
OP = mybir.AluOpType
NPBF = ml_dtypes.bfloat16

NCORES = 8
TOK = 2048
Q = 128
NCH = TOK // Q
D = 1024
KC = D // 128
INNER = 2048
HEADS = 8
HD = 256
ST = 64
PROJ_TM = INNER + ST
NJ = 2048
EPS = 1e-5
NEG = -1.0e30

ENGS = ("pe", "act", "dve", "pool", "sp")
NDL = 8
DMAKEYS = tuple(f"dma_{q}{l}" for q in ("sp", "pool") for l in range(NDL))
SEMKEYS = ENGS + DMAKEYS + ("cc",)


class Buf:
    __slots__ = ("name", "w", "readers")

    def __init__(self, name=""):
        self.name = name
        self.w = None
        self.readers = {}


class Sch:
    def __init__(self):
        self.thunks = {e: [] for e in ENGS}
        self.cnt = {k: 0 for k in SEMKEYS}
        self.dix = {"sp": 0, "pool": 0}
        self.waited = {}
        self.sems = {}

    def _wait_list(self, eng, ins, outs, dma, mykey=None):
        if mykey is None:
            mykey = eng
        need = {}

        def add(sk, v):
            if v is None:
                return
            if (not dma) and sk == eng == "pe":
                return
            if need.get(sk, 0) < v:
                need[sk] = v

        for b in ins:
            if b.w:
                add(*b.w)
        for b in outs:
            if b.w:
                add(*b.w)
            for sk, v in b.readers.items():
                add(sk, v)
        waits = []
        for sk, v in need.items():
            if self.waited.get((eng, sk), 0) < v:
                waits.append((sk, v))
                self.waited[(eng, sk)] = v
        return waits

    def op(self, eng, build, ins=(), outs=(), inc=True, dma=False):
        if dma:
            lane = self.dix[eng] % NDL
            self.dix[eng] += 1
            mykey = f"dma_{eng}{lane}"
        else:
            mykey = eng
        waits = self._wait_list(eng, ins, outs, dma, mykey=mykey)
        if dma:
            prior = self.cnt[mykey]
            if prior > 0 and self.waited.get((eng, mykey), 0) < prior:
                waits.append((mykey, prior))
                self.waited[(eng, mykey)] = prior
            self.cnt[mykey] += 16
            myv, step = self.cnt[mykey], 16
        elif inc:
            self.cnt[mykey] += 1
            myv, step = self.cnt[mykey], 1
        else:
            myv, step = self.cnt[mykey] + 1, 0
        for b in outs:
            b.w = (mykey, myv)
        for b in ins:
            if b.readers.get(mykey, 0) < myv:
                b.readers[mykey] = myv
        sch = self

        def th(engine, waits=tuple(waits), build=build, step=step, mykey=mykey):
            for sk, v in waits:
                engine.wait_ge(sch.sems[sk], v)
            inst = build(engine)
            if step:
                inst.then_inc(sch.sems[mykey], step)

        self.thunks[eng].append(th)

    def barrier(self):
        sch = self
        for e in ENGS:
            pre = self.cnt[e]
            self.cnt[e] += 1

            def th(engine, e=e, pre=pre):
                if pre:
                    engine.wait_ge(sch.sems[e], pre)
                engine.nop().then_inc(sch.sems[e], 1)
            self.thunks[e].append(th)
        snap = dict(self.cnt)
        snap.pop("cc", None)
        self.cnt["dve"] += 1
        bval = self.cnt["dve"]

        pre_dve = snap["dve"]

        def th_dve(engine, snap=dict(snap), pre_dve=pre_dve):
            for sk, v in snap.items():
                if sk != "dve" and v > 0:
                    engine.wait_ge(sch.sems[sk], v)
            engine.wait_ge(sch.sems["dve"], pre_dve)
            engine.nop().then_inc(sch.sems["dve"], 1)
        self.thunks["dve"].append(th_dve)
        for e in ENGS:
            if e == "dve":
                continue

            def th2(engine, bval=bval):
                engine.wait_ge(sch.sems["dve"], bval)
            self.thunks[e].append(th2)
        for e in ENGS:
            for sk, v in snap.items():
                if self.waited.get((e, sk), 0) < v:
                    self.waited[(e, sk)] = v
            if self.waited.get((e, "dve"), 0) < bval:
                self.waited[(e, "dve")] = bval


class Arena:

    def __init__(self, nc, budget_bytes=204 * 1024):
        self.nc = nc
        self.budget = budget_bytes
        self.items = []
        self.handles = {}

    def alloc(self, name, shape, dtype, p0=0, p1=5):
        per = int(np.prod(shape[1:])) * mybir.dt.size(dtype)
        per = (per + 31) // 32 * 32
        cands = sorted(
            (o, sz) for (o, sz, a, b) in self.items if not (b < p0 or a > p1)
        )
        gaps = []
        cur = 0
        for o, sz in cands:
            if o > cur:
                gaps.append((cur, o - cur))
            cur = max(cur, o + sz)
        gaps.append((cur, self.budget - cur))
        fit = sorted((g for g in gaps if g[1] >= per), key=lambda g: g[1])
        assert fit, (
            f"SBUF overflow: {name} {per}B (gaps {gaps[-1]}) > {self.budget}")
        off = fit[0][0]
        self.items.append((off, per, p0, p1))
        h = self.nc.virtual_sbuf_tensor(name, list(shape), dtype)
        self.handles[name] = (h, off)
        return h

    def finalize(self):
        size = max(o + sz for (o, sz, _, _) in self.items)
        big = self.nc.alloc_sbuf_tensor("arena", [128, size], mybir.dt.uint8)
        base = self.nc.lookup_mloc(big).addr
        for name, (h, off) in self.handles.items():
            self.nc._tensor(h.name, h.shape, h.dtype, type="SB")
            mloc = self.nc.lookup_mloc(h)
            mloc.allocated = True
            mloc.addr = base + off
        return size


def _ffn_perm():
    lvl_off = [0, 8, 24, 56, 128, 256, 512, 1024]
    perm = np.full(NJ, -1, np.int64)
    for d in range(8):
        for r in range(8 * (1 << d)):
            tree, bits = r % 8, r // 8
            pos = 0
            for l in range(1, d + 1):
                pos += ((bits >> (l - 1)) & 1) << (d - l)
            perm[lvl_off[d] + r] = tree * 255 + (1 << d) - 1 + pos
    return perm


def _kc_layout(w, cols):
    return np.ascontiguousarray(
        w.reshape(KC, 128, cols).transpose(1, 0, 2).reshape(128, KC * cols))


def _rows_layout(w, rows, cols):
    n = rows // 128
    return np.ascontiguousarray(
        w.reshape(n, 128, cols).transpose(1, 0, 2).reshape(128, n * cols))


def prep_inputs(inputs):
    x = np.asarray(inputs["L0_attn_input_tensor"], np.float32).reshape(16384, D)
    W_in = np.asarray(inputs["L0_attn_W_in"], np.float32)
    b_in = np.asarray(inputs["L0_attn_b_in"], np.float32)
    W_out = np.asarray(inputs["L0_attn_W_out"], np.float32)
    b_out = np.asarray(inputs["L0_attn_b_out"], np.float32)
    Wf_in = np.asarray(inputs["L0_ffn_W_in"], np.float32)
    bf_in = np.asarray(inputs["L0_ffn_b_in"], np.float32)
    Wf_out = np.asarray(inputs["L0_ffn_W_out"], np.float32)

    perm = _ffn_perm()
    ok = perm >= 0
    Winf_p = np.zeros((NJ, D), np.float32)
    Winf_p[ok] = Wf_in[perm[ok]]
    binf_p = np.zeros(NJ, np.float32)
    binf_p[ok] = bf_in[perm[ok]]
    Woutf_p = np.zeros((NJ, D), np.float32)
    Woutf_p[ok] = Wf_out[:, perm[ok]].T

    com = {
        "wproj": _kc_layout(np.hstack([W_in[:, INNER:2 * INNER],
                                       W_in[:, 2 * INNER:2 * INNER + ST]]),
                            PROJ_TM),
        "wbc": _kc_layout(W_in[:, 2 * INNER:2 * INNER + 2 * ST], 2 * ST),
        "wdtf": _kc_layout(np.hstack([W_in[:, 2 * INNER + 2 * ST:],
                                      np.zeros((D, 24), np.float32)]), 32),
        "wz": _kc_layout(W_in[:, :INNER], INNER),
        "wout": _rows_layout(W_out, INNER, D),
        "winf": _kc_layout(np.ascontiguousarray(Winf_p.T), NJ),
        "woutf": _rows_layout(Woutf_p, NJ, D),
        "b_xsb": np.hstack([b_in[INNER:2 * INNER],
                            b_in[2 * INNER:2 * INNER + ST]]).reshape(1, -1),
        "b_bc": b_in[2 * INNER:2 * INNER + 2 * ST].reshape(1, -1).copy(),
        "b_dt": np.hstack([b_in[2 * INNER + 2 * ST:],
                           np.zeros(24, np.float32)]).reshape(1, -1),
        "dtb": np.hstack([np.asarray(inputs["L0_attn_dt_bias"], np.float32),
                          np.zeros(24, np.float32)]).reshape(1, -1),
        "b_z": b_in[:INNER].reshape(1, -1).copy(),
        "b_o": b_out.reshape(1, -1).copy(),
        "b_inf": binf_p.reshape(1, -1),
        "gam": np.asarray(inputs["L0_attn_ln_gamma"], np.float32).reshape(1, D),
        "bet": np.asarray(inputs["L0_attn_ln_beta"], np.float32).reshape(1, D),
        "alog": np.asarray(inputs["L0_attn_A_log"], np.float32).reshape(1, -1),
        "dsk": np.asarray(inputs["L0_attn_D_skip"], np.float32).reshape(1, -1),
        "cid_bf": np.eye(128, dtype=np.float32).astype(NPBF),
        "cid_f": np.eye(128, dtype=np.float32),
        "cones8": np.ones((8, 128), np.float32).astype(NPBF),
        "constf": np.tile(np.array([EPS, 1.0, -1.0, -0.5, 0.0,
                                    1.0 / D, 40.0, 2.0], np.float32),
                          (128, 1)),
        "conesr": np.ones((1, 512), np.float32).astype(NPBF),
    }
    cb0 = np.zeros((8, 512), np.float32)
    cb1 = np.zeros((8, 512), np.float32)
    for hh in range(4):
        cb0[hh, hh * 128:(hh + 1) * 128] = 1.0
        cb1[hh + 4, hh * 128:(hh + 1) * 128] = 1.0
    com["cblk0"] = cb0.astype(NPBF)
    com["cblk1"] = cb1.astype(NPBF)
    mneg = np.zeros((128, 512), np.float32)
    tri = np.tril(np.full((Q, Q), NEG, np.float32), -1)
    for hh in range(4):
        mneg[:, hh * 128:(hh + 1) * 128] = tri
    com["cmneg"] = mneg.astype(NPBF)

    maps = []
    for c in range(NCORES):
        m = np.zeros(8, np.float32)
        m[:c] = 1.0
        mp = dict(com)
        mp["x"] = np.ascontiguousarray(x[c * TOK:(c + 1) * TOK])
        mp["m_col"] = m.reshape(8, 1).copy()
        mp["minv_col"] = (1.0 - m).reshape(8, 1).copy()
        mp["m_row"] = m.reshape(1, 8).astype(NPBF)
        maps.append(mp)
    return maps


def build(ncores=NCORES):
    nc = bass.Bass("TRN2", target_bir_lowering=False, debug=False)
    s = Sch()

    def din(name, shape, dtype=F32):
        return nc.dram_tensor(name, list(shape), dtype, kind="ExternalInput").ap()

    x_d = din("x", (TOK, D))
    wproj_d = din("wproj", (128, KC * PROJ_TM))
    wbc_d = din("wbc", (128, KC * 2 * ST))
    wdtf_d = din("wdtf", (128, KC * 32))
    wz_d = din("wz", (128, KC * INNER))
    wout_d = din("wout", (128, 16 * D))
    winf_d = din("winf", (128, KC * NJ))
    woutf_d = din("woutf", (128, 16 * D))
    b_xsb_d = din("b_xsb", (1, PROJ_TM))
    b_bc_d = din("b_bc", (1, 2 * ST))
    b_dt_d = din("b_dt", (1, 32))
    dtb_d = din("dtb", (1, 32))
    b_z_d = din("b_z", (1, INNER))
    b_o_d = din("b_o", (1, D))
    b_inf_d = din("b_inf", (1, NJ))
    gam_d = din("gam", (1, D))
    bet_d = din("bet", (1, D))
    alog_d = din("alog", (1, HEADS))
    dsk_d = din("dsk", (1, HEADS))
    cid_bf_d = din("cid_bf", (128, 128), BF16)
    cid_f_d = din("cid_f", (128, 128))
    cones8_d = din("cones8", (8, 128), BF16)
    constf_d = din("constf", (128, 8))
    conesr_d = din("conesr", (1, 512), BF16)
    cblk0_d = din("cblk0", (8, 512), BF16)
    cblk1_d = din("cblk1", (8, 512), BF16)
    cmneg_d = din("cmneg", (128, 512), BF16)
    mcol_d = din("m_col", (8, 1))
    minv_d = din("minv_col", (8, 1))
    mrow_d = din("m_row", (1, 8), BF16)

    y_d = nc.dram_tensor("y", [TOK, D], F32, kind="ExternalOutput").ap()
    xs_h = nc.dram_tensor("xs_h", [TOK, INNER], BF16).ap()
    xn_h = nc.dram_tensor("xn_h", [128, KC * TOK], BF16).ap()
    xn2_h = nc.dram_tensor("xn2_h", [128, KC * TOK], BF16).ap()
    part_h = nc.dram_tensor("part_h", [TOK, D], F32).ap()
    em_h = nc.dram_tensor("em_h", [8, 8], F32).ap()
    es_h = nc.dram_tensor("es_h", [16, 8], BF16).ap()
    cc_in = nc.dram_tensor("cc_in", [ST + 1, TOK], F32).ap()
    cc_out = nc.dram_tensor("cc_out", [(ST + 1) * ncores, TOK], F32,
                            addr_space="Shared").ap()

    ar = Arena(nc)
    A = ar.alloc
    cid_b = A("cid_b", [128, 128], BF16, 0, 5)
    cid_f = A("cid_f", [128, 128], F32, 0, 5)
    cones8s = A("cones8s", [8, 128], BF16, 0, 5)
    constf = A("constf", [128, 8], F32, 0, 5)
    conesrs = A("conesrs", [1, 512], BF16, 0, 5)
    cblk0s = A("cblk0s", [8, 512], BF16, 0, 5)
    cblk1s = A("cblk1s", [8, 512], BF16, 0, 5)
    cmnegs = A("cmnegs", [128, 512], BF16, 0, 5)
    idsk = A("idsk", [128, 1024], BF16, 0, 5)
    acol = A("acol", [8, 1], F32, 0, 1)
    stat_rs = A("stat_rs", [128, 16], F32, 0, 2)
    stat_nmr = A("stat_nmr", [128, 16], F32, 0, 2)
    mcol_s = A("mcol_s", [8, 1], F32, 0, 3)
    minv_s = A("minv_s", [8, 1], F32, 0, 3)
    mrow_s = A("mrow_s", [1, 8], BF16, 0, 3)
    xn2_bf = A("xn2_bf", [128, KC * TOK], BF16, 0, 1)
    pk_hi = A("pk_hi", [8, TOK], BF16, 1, 5)
    pk_lo = A("pk_lo", [8, TOK], BF16, 1, 5)
    pg_hi = A("pg_hi", [8, TOK], BF16, 1, 5)
    pg_lo = A("pg_lo", [8, TOK], BF16, 1, 5)
    ecs_bf = A("ecs_bf", [8, TOK], BF16, 1, 5)
    B_tm = A("B_tm", [128, NCH * ST], BF16, 1, 5)
    B_fm = A("B_fm", [64, NCH * Q], BF16, 1, 5)
    C_fm = A("C_fm", [64, NCH * Q], BF16, 1, 5)
    eSrow_hi = A("eSrow_hi", [1, 128], BF16, 1, 5)
    eSrow_lo = A("eSrow_lo", [1, 128], BF16, 1, 5)
    slog16 = A("slog16", [8, 16], F32, 1, 5)
    scq = A("scq", [8, 128], F32, 1, 5)
    tmq = A("tmq", [128, 16], F32, 1, 5)
    bsc_all = [A(f"bsc_all{i}", [128, 8 * ST], BF16, 1, 1)
               for i in range(2)]
    wproj_s = A("wproj_s", [128, KC * PROJ_TM], BF16, 0, 1)
    wbc_s = A("wbc_s", [128, KC * 2 * ST], BF16, 0, 1)
    wdtf_s = A("wdtf_s", [128, KC * 32], BF16, 0, 1)
    winf_s = A("winf_s", [128, KC * NJ], BF16, 0, 2)
    brow_xsb = A("brow_xsb", [1, PROJ_TM], BF16, 0, 1)
    brow_bc = A("brow_bc", [1, 2 * ST], BF16, 0, 1)
    brow_dt = A("brow_dt", [1, 32], BF16, 0, 1)
    brow_dtb = A("brow_dtb", [1, 32], BF16, 0, 1)
    brow_inf = A("brow_inf", [1, NJ], BF16, 0, 2)
    gam_s = A("gam_s", [128, D], F32, 0, 0)
    bet_s = A("bet_s", [128, D], F32, 0, 0)
    dbc = A("dbc", [128, 8], F32, 0, 0)
    dskb = A("dskb", [1, 8], BF16, 0, 0)
    NP0 = 4
    xt = [A(f"xt{i}", [128, D], F32, 0, 0) for i in range(NP0)]
    ln_a = [A(f"ln_a{i}", [128, D], F32, 0, 0) for i in range(NP0)]
    ln_b = [A(f"ln_b{i}", [128, D], F32, 0, 0) for i in range(NP0)]
    ln_c = [A(f"ln_c{i}", [128, D], F32, 0, 0) for i in range(NP0)]
    ln_st = [A(f"ln_st{i}", [128, 8], F32, 0, 0) for i in range(NP0)]
    xnst = [A(f"xnst{i}", [128, D], BF16, 0, 0) for i in range(NP0)]
    sA = A("sA", [8, TOK], F32, 1, 1)
    sB = A("sB", [8, TOK], F32, 1, 1)
    sC = A("sC", [8, TOK], F32, 1, 1)
    cs = A("cs", [8, 1 + TOK], F32, 1, 1)
    esc16 = A("esc16", [8, 32], F32, 1, 1)
    esc16b = A("esc16b", [16, 8], BF16, 1, 1)
    xsb = [A(f"xsb{i}", [128, INNER], BF16, 1, 1) for i in range(3)]
    bsc_tm = A("bsc_tm", [128, NCH * 8], F32, 1, 5)
    hloc_sb = A("hloc_sb", [ST, HEADS * HD], F32, 1, 1)
    xn_bf = A("xn_bf", [128, KC * TOK], BF16, 2, 2)
    woutf_s = A("woutf_s", [128, 16 * D], BF16, 2, 2)
    silzfA = [A(f"silzfA{i}", [128, 16 * 256], BF16, 2, 2) for i in range(2)]
    decA = [A(f"decA{i}", [128, 7 * 256], BF16, 2, 2) for i in range(2)]
    selA = [A(f"selA{i}", [128, 16 * 256], BF16, 2, 2) for i in range(2)]
    l0f32 = [A(f"l0f32{i}", [128, 256], F32, 2, 2)
             for i in range(2)]
    dectm = A("dectm", [128, 120], BF16, 2, 2)
    seltm = A("seltm", [128, 128], BF16, 2, 2)
    sel4tm = A("sel4tm", [128, 128], BF16, 2, 2)
    oa = [A(f"oa{i}", [128, D], F32, 2, 2) for i in range(2)]
    xta = [A(f"xta{i}", [128, D], F32, 2, 2) for i in range(2)]
    hglob = A("hglob", [ST, HEADS * HD], F32, 3, 5)
    hglob_bf = A("hglob_bf", [ST, HEADS * HD], BF16, 3, 5)
    hload = [A(f"hload{i}", [ST, HEADS * HD], F32, 3, 3) for i in range(2)]
    mh_x = A("mh_x", [ST, HEADS * HD], F32, 3, 3)
    sg8 = A("sg8", [8, 8], F32, 3, 3)
    emod8 = A("emod8", [8, 8], F32, 3, 3)
    emrow_b = A("emrow_b", [1, 64], BF16, 3, 3)
    embc = A("embc", [ST, 8], F32, 3, 3)
    mbc = A("mbc", [ST, 8], F32, 3, 3)
    wzp = [A("wzp0", [128, KC * 1024], BF16, 3, 4),
           A("wzp1", [128, KC * 1024], BF16, 4, 5)]
    woutp = [A("woutp0", [128, 8 * D], BF16, 3, 4),
             A("woutp1", [128, 8 * D], BF16, 4, 5)]
    brow_z = A("brow_z", [1, INNER], BF16, 2, 5)
    brow_o = A("brow_o", [1, D], BF16, 2, 5)
    xn2sg = [A(f"xn2sg{i}", [128, KC * 512], BF16, 3, 5) for i in range(2)]
    silz8 = A("silz8", [128, 8 * 1024], BF16, 4, 5)
    xsc = [A(f"xsc{i}", [128, 1024], BF16, 4, 5) for i in range(3)]
    ptc = [A(f"ptc{i}", [128, D], F32, 4, 5) for i in range(2)]
    ebh = [A(f"ebh{i}", [8, 512], BF16, 4, 5) for i in range(2)]
    ebl = [A(f"ebl{i}", [8, 512], BF16, 4, 5) for i in range(2)]
    ebc = [A(f"ebc{i}", [8, 512], BF16, 4, 5) for i in range(2)]
    ebE = [A(f"ebE{i}", [128, 512], BF16, 4, 5) for i in range(2)]
    wch = [A(f"wch{i}", [128, 512], BF16, 4, 5) for i in range(2)]
    cecs = [A(f"cecs{i}", [ST, 512], BF16, 4, 5) for i in range(2)]
    bsc2 = [A(f"bsc2{i}", [128, 4 * ST], BF16, 4, 5) for i in range(2)]
    gbfA = A("gbfA", [128, NCH * 128], BF16, 3, 5)
    eSbcA = A("eSbcA", [ST, NCH * 8], F32, 3, 5)
    scC = A("scC", [8, TOK], F32, 3, 3)
    ygb = [A(f"ygb{i}", [128, 8 * Q], BF16, 4, 5) for i in range(2)]

    ar.finalize()

    pb = [nc.alloc_psum_tensor(f"pb{k}", [128, 512], F32) for k in range(8)]
    pB = [Buf(f"pb{k}") for k in range(8)]

    bufs = {}

    def W(h):
        if h.name not in bufs:
            bufs[h.name] = Buf(h.name)
        return bufs[h.name]

    b_xsb = [Buf(f"xsb{i}") for i in range(3)]
    b_xsc = [Buf(f"xsc{i}") for i in range(3)]
    b_xt = [Buf(f"xt{i}") for i in range(4)]
    b_xnst = [Buf(f"xnst{i}") for i in range(4)]
    b_xta = [Buf(f"xta{i}") for i in range(2)]
    b_ptc = [Buf(f"ptc{i}") for i in range(2)]
    b_oa = [Buf(f"oa{i}") for i in range(2)]
    b_ygb = [Buf(f"ygb{i}") for i in range(2)]
    b_hload = [Buf(f"hload{i}") for i in range(2)]
    b_xn2sg = [Buf(f"xn2sg{i}") for i in range(2)]
    b_y = Buf("y_dram")
    b_part = [Buf(f"part{i}") for i in range(16)]
    b_xsh = [Buf(f"xs_h{c}") for c in range(16)]
    b_xnh = Buf("xn_h")
    b_xn2h = Buf("xn2_h")
    b_ccin = Buf("cc_in")
    b_ccout = Buf("cc_out")
    b_emh = Buf("em_h")
    b_esh = Buf("es_h")

    def dma(q, out, in_, ins=(), outs=()):
        s.op(q, lambda e, o=out, i=in_: e.dma_start(out=o, in_=i),
             ins=ins, outs=outs, dma=True)

    def mm(out, lhsT, rhs, start, stop, ins=(), outs=(), inc=None):
        s.op("pe",
             lambda e, o=out, l=lhsT, r=rhs, a=start, b=stop:
                 e.matmul(o, l, r, start=a, stop=b),
             ins=ins, outs=outs, inc=(stop if inc is None else inc))

    def act(out, in_, func, bias=None, scale=None, ins=(), outs=()):
        kw = {}
        if bias is not None:
            kw["bias"] = bias
        if scale is not None:
            kw["scale"] = scale
        if kw and func == AF.Copy:
            func = AF.Identity
        s.op("act",
             lambda e, o=out, i=in_, f=func, k=dict(kw): e.activation(o, i, f, **k),
             ins=ins, outs=outs)

    def tt(out, a, b, op_, ins=(), outs=()):
        s.op("dve", lambda e, o=out, p=a, q=b, r=op_: e.tensor_tensor(o, p, q, r),
             ins=ins, outs=outs)

    def ts_(out, in_, s1, op0, s2=None, op1=None, ins=(), outs=()):
        def bld(e, o=out, i=in_, a=s1, b=s2, p0=op0, p1=op1):
            if p1 is None:
                return e.tensor_scalar(o, i, a, None, p0)
            return e.tensor_scalar(o, i, a, b, p0, p1)
        s.op("dve", bld, ins=ins, outs=outs)

    def stt(out, in0, sc, in1, op0, op1, ins=(), outs=()):
        s.op("dve",
             lambda e, o=out, i0=in0, c=sc, i1=in1, p0=op0, p1=op1:
                 e.scalar_tensor_tensor(o, i0, c, i1, p0, p1),
             ins=ins, outs=outs)

    def ttr(out, in0, in1, accum, ins=(), outs=()):
        s.op("dve",
             lambda e, o=out, a=in0, b=in1, ac=accum:
                 e.tensor_tensor_reduce(o, a, b, 1.0, 0.0, OP.mult, OP.add, ac),
             ins=ins, outs=outs)

    def copy(out, in_, ins=(), outs=()):
        s.op("dve", lambda e, o=out, i=in_: e.tensor_copy(o, i),
             ins=ins, outs=outs)

    def trn(outp, in_, ident, ins=(), outs=()):
        s.op("pe", lambda e, o=outp, i=in_, d=ident: e.transpose(o, i, d),
             ins=ins, outs=outs)

    def red(out, in_, ins=(), outs=()):
        s.op("dve",
             lambda e, o=out, i=in_:
                 e.tensor_reduce(o, i, mybir.AxisListType.X, OP.add),
             ins=ins, outs=outs)

    def memset(buf_ap, val, ins=(), outs=()):
        s.op("dve", lambda e, o=buf_ap, v=val: e.memset(o, v),
             ins=ins, outs=outs)

    from concourse.hw_specs import get_activation_tables
    _tab_names = list(get_activation_tables(nc.m.arch))

    def table(name):
        idx = _tab_names.index(name)

        def bld(e, idx=idx):
            return e.add_instruction(
                mybir.InstLoadActFuncSet(
                    name=nc.get_next_instruction_name(),
                    act_func_set_id=idx, ins=[], outs=[]))
        s.op("act", bld, inc=False)

    def pbf(k):
        return pb[k].ap().bitcast(BF16)

    dma("sp", cid_b[:, :], cid_bf_d, outs=(W(cid_b),))
    dma("sp", cid_f[:, :], cid_f_d, outs=(W(cid_f),))
    dma("sp", cones8s[:, :], cones8_d, outs=(W(cones8s),))
    dma("sp", constf[:, :], constf_d, outs=(W(constf),))
    for _ci, _cv in enumerate([EPS, 1.0, -1.0, -0.5, 0.0, 1.0 / D, 40.0, 2.0]):
        nc.const_aps.register(F32, _cv, constf[:, _ci:_ci + 1])
    dma("sp", conesrs[:, :], conesr_d, outs=(W(conesrs),))
    dma("sp", cblk0s[:, :], cblk0_d, outs=(W(cblk0s),))
    dma("sp", cblk1s[:, :], cblk1_d, outs=(W(cblk1s),))
    dma("sp", cmnegs[:, :], cmneg_d, outs=(W(cmnegs),))
    dma("sp", gam_s[:, :],
        bass.AP(gam_d.tensor, 0, [[0, 128], [1, D]]), outs=(W(gam_s),))
    dma("sp", bet_s[:, :],
        bass.AP(bet_d.tensor, 0, [[0, 128], [1, D]]), outs=(W(bet_s),))
    dma("sp", mcol_s[:, :], mcol_d, outs=(W(mcol_s),))
    dma("sp", minv_s[:, :], minv_d, outs=(W(minv_s),))
    dma("sp", mrow_s[:, :], mrow_d, outs=(W(mrow_s),))
    dma("sp", acol[:, :], alog_d.rearrange("o h -> h o"), outs=(W(acol),))
    table("natural_log_exp_and_others")
    act(acol[:, :], acol[:, :], AF.Exp, ins=(W(acol), W(constf)), outs=(W(acol),))
    ts_(acol[:, :], acol[:, :], -1.0, OP.mult, ins=(W(acol),), outs=(W(acol),))
    dma("pool", brow_xsb[:, :], b_xsb_d, outs=(W(brow_xsb),))
    dma("pool", brow_bc[:, :], b_bc_d, outs=(W(brow_bc),))
    dma("pool", brow_dt[:, :], b_dt_d, outs=(W(brow_dt),))
    dma("pool", brow_dtb[:, :], dtb_d, outs=(W(brow_dtb),))
    dma("pool", brow_inf[:, :], b_inf_d, outs=(W(brow_inf),))
    dma("pool", dskb[:, :], dsk_d, outs=(W(dskb),))
    mm(pb[7][:, 0:8], conesrs[0:1, 0:128], dskb[:, :], True, True,
       ins=(W(conesrs), W(dskb)), outs=(pB[7],))
    copy(dbc[:, :], pb[7][:, 0:8], ins=(pB[7], W(constf)), outs=(W(dbc),))
    tt(idsk.ap().rearrange("p (a b) -> p a b", a=8),
       cid_f.ap().unsqueeze(1).broadcast_to([128, 8, 128]),
       dbc.ap().unsqueeze(2).broadcast_to([128, 8, 128]),
       OP.mult, ins=(W(cid_f), W(dbc)), outs=(W(idsk),))

    b_lna = [Buf(f"lna{i}") for i in range(4)]
    b_lnb = [Buf(f"lnb{i}") for i in range(4)]
    b_lnc = [Buf(f"lnc{i}") for i in range(4)]
    b_lst = [Buf(f"lst{i}") for i in range(4)]

    def p0_stage(fn, tiles):
        for i in tiles:
            par = i % 4
            fn(i, par, xt[par], ln_a[par], ln_b[par], ln_c[par], ln_st[par],
               b_xt[par], b_lna[par], b_lnb[par], b_lnc[par], b_lst[par])

    p0_stages = [
        lambda i, par, x, la, lb, lc, st_, xb, ba, bb_, bc_, bst:
            dma("sp", x[:, :], x_d[i * 128:(i + 1) * 128, :],
                ins=(xb,), outs=(xb,)),
        lambda i, par, x, la, lb, lc, st_, xb, ba, bb_, bc_, bst: (
            red(st_[:, 0:1], x[:, :], ins=(xb,), outs=(bst,)),
            ts_(st_[:, 1:2], st_[:, 0:1], 1.0 / D, OP.mult,
                ins=(bst,), outs=(bst,)),
            ts_(la[:, :], x[:, :], st_[:, 1:2], OP.subtract,
                ins=(xb, bst), outs=(ba,))),
        lambda i, par, x, la, lb, lc, st_, xb, ba, bb_, bc_, bst:
            act(lb[:, :], la[:, :], AF.Square, ins=(ba,), outs=(bb_,)),
        lambda i, par, x, la, lb, lc, st_, xb, ba, bb_, bc_, bst: (
            red(st_[:, 2:3], lb[:, :], ins=(bb_,), outs=(bst,)),
            ts_(st_[:, 3:4], st_[:, 2:3], 1.0 / D, OP.mult,
                ins=(bst,), outs=(bst,))),
        lambda i, par, x, la, lb, lc, st_, xb, ba, bb_, bc_, bst: (
            act(st_[:, 4:5], st_[:, 3:4], AF.Ln, bias=EPS,
                ins=(bst,), outs=(bst,)),
            act(st_[:, 4:5], st_[:, 4:5], AF.Exp, scale=-0.5,
                ins=(bst,), outs=(bst,)),
            ts_(lb[:, :], la[:, :], st_[:, 4:5], OP.mult,
                ins=(ba, bst), outs=(bb_,))),
        lambda i, par, x, la, lb, lc, st_, xb, ba, bb_, bc_, bst: (
            copy(stat_rs[:, i:i + 1], st_[:, 4:5], ins=(bst,),
                 outs=(W(stat_rs),)),
            stt(stat_nmr[:, i:i + 1], st_[:, 1:2], -1.0, st_[:, 4:5],
                OP.mult, OP.mult, ins=(bst,), outs=(W(stat_nmr),)),
            stt(st_[:, 5:6], st_[:, 4:5], st_[:, 4:5], st_[:, 3:4],
                OP.mult, OP.mult, ins=(bst,), outs=(bst,))),
        lambda i, par, x, la, lb, lc, st_, xb, ba, bb_, bc_, bst: (
            act(st_[:, 6:7], st_[:, 5:6], AF.Ln, bias=EPS,
                ins=(bst,), outs=(bst,)),
            act(st_[:, 6:7], st_[:, 6:7], AF.Exp, scale=-0.5,
                ins=(bst,), outs=(bst,))),
        lambda i, par, x, la, lb, lc, st_, xb, ba, bb_, bc_, bst: (
            tt(st_[:, 7:8], st_[:, 4:5], st_[:, 6:7], OP.mult,
               ins=(bst,), outs=(bst,))),
        lambda i, par, x, la, lb, lc, st_, xb, ba, bb_, bc_, bst:
            ts_(lc[:, :], la[:, :], st_[:, 7:8], OP.mult,
                ins=(ba, bst), outs=(bc_,)),
        lambda i, par, x, la, lb, lc, st_, xb, ba, bb_, bc_, bst: (
            tt(lc[:, :], lc[:, :], gam_s[:, :], OP.mult,
               ins=(bc_, W(gam_s)), outs=(bc_,)),
            tt(lc[:, :], lc[:, :], bet_s[:, :], OP.add,
               ins=(bc_, W(bet_s)), outs=(bc_,))),
    ]
    for i in range(16):

        def p0_trn(i, par, x, la, lb, lc, st_, xb, ba, bb_, bc_, bst):
            nb = b_xnst[par]
            for half in range(2):
                for q in range(2):
                    bank = half * 2 + q
                    for j in range(2):
                        kc = half * 4 + q * 2 + j
                        trn(pb[bank][:, j * 128:(j + 1) * 128],
                            lb[:, kc * 128:(kc + 1) * 128],
                            cid_f[:, :], ins=(bb_, W(cid_f)), outs=(pB[bank],))
                    act(xnst[par].ap().rearrange("p (k t) -> p k t", k=KC)
                        [:, half * 4 + q * 2: half * 4 + q * 2 + 2, :],
                        pb[bank][:, 0:256].rearrange("p (k t) -> p k t", k=2),
                        AF.Copy, ins=(pB[bank],), outs=(nb,))
            for half in range(2):
                for q in range(2):
                    bank = 4 + half * 2 + q if half * 2 + q < 3 else 7
                    for j in range(2):
                        kc = half * 4 + q * 2 + j
                        trn(pb[bank][:, j * 128:(j + 1) * 128],
                            lc[:, kc * 128:(kc + 1) * 128],
                            cid_f[:, :], ins=(bc_, W(cid_f)), outs=(pB[bank],))
                    copy(xn2_bf.ap().rearrange("p (k t) -> p k t", k=KC)
                         [:, half * 4 + q * 2: half * 4 + q * 2 + 2,
                          i * 128:(i + 1) * 128],
                         pb[bank][:, 0:256].rearrange("p (k t) -> p k t", k=2),
                         ins=(pB[bank],), outs=(W(xn2_bf),))
            dma("sp",
                xn_h.rearrange("p (k t) -> p k t", k=KC)
                    [:, :, i * 128:(i + 1) * 128],
                xnst[par].ap().rearrange("p (k t) -> p k t", k=KC),
                ins=(nb,), outs=(b_xnh,))
        for st_fn in p0_stages:
            p0_stage(st_fn, [i])
        p0_stage(p0_trn, [i])
        if i == 3:
            dma("pool", wproj_s[:, :], wproj_d, ins=(b_xt[3],),
                outs=(W(wproj_s),))
            dma("pool", wbc_s[:, :], wbc_d, outs=(W(wbc_s),))
            dma("pool", wdtf_s[:, :], wdtf_d, outs=(W(wdtf_s),))
        if i == 7:
            dma("pool", winf_s[:, :], winf_d, ins=(b_xt[3],),
                outs=(W(winf_s),))

    s.barrier()

    bA, bB, bC, bCS = W(sA), W(sB), W(sC), W(cs)
    dma("sp", xn2_h, xn2_bf[:, :], ins=(W(xn2_bf),), outs=(b_xn2h,))
    for gi in range(4):
        pk = 6
        for kc in range(KC):
            mm(pb[pk][0:32, 0:512],
               wdtf_s[:, kc * 32:(kc + 1) * 32],
               xn2_bf[:, kc * TOK + gi * 512: kc * TOK + (gi + 1) * 512],
               kc == 0, False, ins=(W(wdtf_s), W(xn2_bf)), outs=(pB[pk],))
        mm(pb[pk][0:32, 0:512], brow_dt[:, :], conesrs[:, :], False, False,
           ins=(W(brow_dt), W(conesrs)), outs=(pB[pk],))
        mm(pb[pk][0:32, 0:512], brow_dtb[:, :], conesrs[:, :], False, True,
           ins=(W(brow_dtb), W(conesrs)), outs=(pB[pk],))
        copy(sA[:, gi * 512:(gi + 1) * 512], pb[pk][0:8, 0:512],
             ins=(pB[pk],), outs=(bA,))
    for gi in range(4):
        for half, dst in ((0, B_fm), (1, C_fm)):
            pk = 6 + half
            for kc in range(KC):
                mm(pb[pk][0:ST, 0:512],
                   wbc_s[:, kc * 2 * ST + half * ST: kc * 2 * ST + (half + 1) * ST],
                   xn2_bf[:, kc * TOK + gi * 512: kc * TOK + (gi + 1) * 512],
                   kc == 0, False, ins=(W(wbc_s), W(xn2_bf)), outs=(pB[pk],))
            mm(pb[pk][0:ST, 0:512], brow_bc[:, half * ST:(half + 1) * ST],
               conesrs[:, :], False, True,
               ins=(W(brow_bc), W(conesrs)), outs=(pB[pk],))
            for cc in range(4):
                c = gi * 4 + cc
                act(dst[:, c * Q:(c + 1) * Q],
                    pb[pk][0:ST, cc * 128:(cc + 1) * 128],
                    AF.Copy, ins=(pB[pk],), outs=(W(dst),))
    act(sB[:, :], sA[:, :], AF.Abs, ins=(bA,), outs=(bB,))
    act(sB[:, :], sB[:, :], AF.Exp, scale=-1.0, ins=(bB,), outs=(bB,))
    act(sB[:, :], sB[:, :], AF.Ln, bias=1.0, ins=(bB,), outs=(bB,))
    ts_(sC[:, :], sA[:, :], 0.0, OP.max, ins=(bA,), outs=(bC,))
    tt(sA[:, :], sC[:, :], sB[:, :], OP.add, ins=(bC, bB), outs=(bA,))
    act(sB[:, :], sA[:, :], AF.Ln, ins=(bA,), outs=(bB,))
    ts_(sC[:, :], sA[:, :], acol[:, 0:1], OP.mult,
        ins=(bA, W(acol)), outs=(bC,))
    memset(cs[:, 0:1], 0.0, outs=(bCS,))
    s.op("dve",
         lambda e: e.tensor_tensor_scan(cs[:, 1:1 + TOK], sC[:, :],
                                        sC[:, :], 0.0, OP.add, OP.bypass),
         ins=(bC,), outs=(bCS,))
    for c in range(NCH):
        ts_(sC[:, c * Q:(c + 1) * Q], cs[:, 1 + c * Q:1 + (c + 1) * Q],
            cs[:, c * Q:c * Q + 1], OP.subtract, ins=(bCS,), outs=(bC,))
    copy(pk_hi[:, :], sC[:, :], ins=(bC,), outs=(W(pk_hi),))
    tt(pk_lo[:, :], sC[:, :], pk_hi[:, :], OP.subtract,
       ins=(bC, W(pk_hi)), outs=(W(pk_lo),))
    act(ecs_bf[:, :], sC[:, :], AF.Exp, ins=(bC,), outs=(W(ecs_bf),))
    tt(sC[:, :], sB[:, :], sC[:, :], OP.subtract, ins=(bB, bC), outs=(bC,))
    copy(pg_hi[:, :], sC[:, :], ins=(bC,), outs=(W(pg_hi),))
    tt(pg_lo[:, :], sC[:, :], pg_hi[:, :], OP.subtract,
       ins=(bC, W(pg_hi)), outs=(W(pg_lo),))
    ts_(sB[:, :], cs[:, 1:1 + TOK], cs[:, TOK:TOK + 1], OP.subtract,
        ins=(bCS,), outs=(bB,))
    act(sB[:, :], sB[:, :], AF.Exp, scale=-1.0, ins=(bB,), outs=(bB,))
    tt(sB[:, :], sB[:, :], sA[:, :], OP.mult, ins=(bB, bA), outs=(bB,))
    memset(hloc_sb[:, :], 0.0, outs=(W(hloc_sb),))

    def b_state(c):
        if c > 0:
            for h2 in range(4):
                tt(hloc_sb[:, h2 * 512:(h2 + 1) * 512],
                   hloc_sb[:, h2 * 512:(h2 + 1) * 512],
                   pb[2 + h2][0:ST, 0:512], OP.add,
                   ins=(W(hloc_sb), pB[2 + h2]), outs=(W(hloc_sb),))
        tt(bsc_all[c % 2].ap().rearrange("p (a b) -> p a b", a=8),
           B_tm.ap()[:, c * ST:(c + 1) * ST].unsqueeze(1)
               .broadcast_to([128, 8, ST]),
           bsc_tm.ap()[:, c * 8:(c + 1) * 8].unsqueeze(2)
               .broadcast_to([128, 8, ST]),
           OP.mult, ins=(W(B_tm), W(bsc_tm)), outs=(W(bsc_all[c % 2]),))
        for h in range(HEADS):
            bank = 2 + h // 2
            col = (h % 2) * 256
            mm(pb[bank][0:ST, col:col + 256],
               bsc_all[c % 2][:, h * ST:(h + 1) * ST],
               xsb[c % 3][:, h * HD:(h + 1) * HD],
               h % 2 == 0, h % 2 == 1,
               ins=(W(bsc_all[c % 2]), xb_list[c % 3]), outs=(pB[bank],),
               inc=(h % 2 == 1))

    xb_list = b_xsb
    for c in range(NCH):
        if c == 3:
            for c2 in range(NCH):
                pk7 = 6 + c2 % 2
                trn(pb[pk7][:, 128:136], sB[:, c2 * Q:(c2 + 1) * Q],
                    cid_f[0:8, 0:8], ins=(bB, W(cid_f)), outs=(pB[pk7],))
                copy(bsc_tm[:, c2 * 8:(c2 + 1) * 8], pb[pk7][:, 128:136],
                     ins=(pB[pk7],), outs=(W(bsc_tm),))
        if c >= 3:
            b_state(c - 3)
        xb = b_xsb[c % 3]
        for seg in range(5):
            w0 = seg * 512
            wid = 512 if seg < 4 else ST
            pk = seg % 2
            for kc in range(KC):
                mm(pb[pk][:, 0:wid],
                   xn2_bf[:, kc * TOK + c * Q: kc * TOK + (c + 1) * Q],
                   wproj_s[:, kc * PROJ_TM + w0: kc * PROJ_TM + w0 + wid],
                   kc == 0, False, ins=(W(xn2_bf), W(wproj_s)), outs=(pB[pk],))
            mm(pb[pk][:, 0:wid], conesrs[0:1, 0:128],
               brow_xsb[:, w0:w0 + wid], False, True,
               ins=(W(conesrs), W(brow_xsb)), outs=(pB[pk],))
            if seg < 4:
                act(xsb[c % 3][:, w0:w0 + 512], pb[pk][:, 0:512], AF.Copy,
                    ins=(pB[pk],), outs=(xb,))
            else:
                act(B_tm[:, c * ST:(c + 1) * ST], pb[pk][:, 0:ST], AF.Copy,
                    ins=(pB[pk],), outs=(W(B_tm),))
        dma("sp", xs_h[c * 128:(c + 1) * 128, :], xsb[c % 3][:, :],
            ins=(xb,), outs=(b_xsh[c],))
    tt(slog16[:, :], cs.ap()[:, 128::128], cs.ap()[:, 0:TOK:128], OP.subtract,
       ins=(bCS,), outs=(W(slog16),))
    memset(esc16[:, 16:32], 0.0, outs=(W(esc16),))
    act(esc16[:, 0:16], slog16[:, :], AF.Exp, ins=(W(slog16), W(esc16)),
        outs=(W(esc16),))
    trn(pb[7][0:32, 0:8], esc16[:, :], cid_f[0:8, 0:8],
        ins=(W(esc16), W(cid_f)), outs=(pB[7],))
    act(esc16b[:, :], pb[7][0:16, 0:8], AF.Copy, ins=(pB[7],), outs=(W(esc16b),))
    copy(tmq[0:16, 0:8], pb[7][0:16, 0:8], ins=(pB[7],), outs=(W(tmq),))
    dma("sp", es_h, esc16b[:, :], ins=(W(esc16b),), outs=(b_esh,))
    dma("sp", eSrow_hi[:, :], es_h.rearrange("c h -> () (c h)"),
        ins=(b_esh,), outs=(W(eSrow_hi),))
    tt(tmq[0:16, 0:8], tmq[0:16, 0:8], esc16b[:, :], OP.subtract,
       ins=(W(tmq), W(esc16b)), outs=(W(tmq),))
    copy(esc16b[:, :], tmq[0:16, 0:8], ins=(W(tmq),), outs=(W(esc16b),))
    dma("sp", es_h, esc16b[:, :], ins=(W(esc16b),), outs=(b_esh,))
    dma("sp", eSrow_lo[:, :], es_h.rearrange("c h -> () (c h)"),
        ins=(b_esh,), outs=(W(eSrow_lo),))
    for ct in range(NCH - 3, NCH):
        b_state(ct)
    for h2 in range(4):
        tt(hloc_sb[:, h2 * 512:(h2 + 1) * 512],
           hloc_sb[:, h2 * 512:(h2 + 1) * 512],
           pb[2 + h2][0:ST, 0:512], OP.add,
           ins=(W(hloc_sb), pB[2 + h2]), outs=(W(hloc_sb),))
    dma("sp", cc_in[0:ST, :], hloc_sb[:, :], ins=(W(hloc_sb),), outs=(b_ccin,))
    memset(sA[0:1, :], 0.0, ins=(bA,), outs=(bA,))
    dma("sp", cc_in[ST:ST + 1, :], sA[0:1, :], ins=(bA,), outs=(b_ccin,))
    dma("sp", cc_in[ST:ST + 1, 0:8].rearrange("o h -> h o"),
        cs[:, TOK:TOK + 1], ins=(bCS,), outs=(b_ccin,))

    ccw = s._wait_list("pool", (b_ccin,), (b_ccout,), False)
    s.cnt["cc"] += 1
    b_ccout.w = ("cc", s.cnt["cc"])

    def th_cc(engine, waits=tuple(ccw)):
        for sk, v in waits:
            engine.wait_ge(s.sems[sk], v)
        engine.collective_compute(
            "AllGather", OP.bypass,
            replica_groups=[list(range(ncores))],
            ins=[cc_in.opt()], outs=[cc_out.opt()],
        ).then_inc(s.sems["cc"], 1)
    s.thunks["pool"].append(th_cc)

    s.barrier()

    table("silu_and_others")
    memset(seltm[:, 0:8], 1.0, outs=(W(seltm),))
    memset(seltm[:, 120:128], 0.0, ins=(W(seltm),), outs=(W(seltm),))
    b_xnq = [Buf(f"xnq{qd}") for qd in range(4)]
    for qd in range(4):
        for k2 in range(4):
            dma("sp",
                xn_bf.ap().rearrange("p (k t) -> p k t", k=KC)
                     [:, k2 * 2:k2 * 2 + 2, qd * 512:(qd + 1) * 512],
                xn_h.rearrange("p (k t) -> p k t", k=KC)
                    [:, k2 * 2:k2 * 2 + 2, qd * 512:(qd + 1) * 512],
                ins=(b_xnh,), outs=(b_xnq[qd],))
    dma("pool", woutf_s[:, :], woutf_d, ins=(b_xnq[0],),
        outs=(W(woutf_s),))
    bsilL = [W(silzfA[i]) for i in range(2)]
    bdecL = [W(decA[i]) for i in range(2)]
    bselL = [W(selA[i]) for i in range(2)]

    def a_logits(gi):
        gp = gi % 2
        t0 = gi * 256
        bsil, bdec = bsilL[gp], bdecL[gp]
        for jc in range(16):
            pk = jc % 2
            for kc in range(KC):
                mm(pb[pk][:, 0:256],
                   winf_s[:, kc * NJ + jc * 128: kc * NJ + (jc + 1) * 128],
                   xn_bf[:, kc * TOK + t0: kc * TOK + t0 + 256],
                   kc == 0, False, ins=(W(winf_s), b_xnq[gi // 2]),
                   outs=(pB[pk],))
            mm(pb[pk][:, 0:256], brow_inf[:, jc * 128:(jc + 1) * 128],
               conesrs[0:1, 0:256], False, True,
               ins=(W(brow_inf), W(conesrs)), outs=(pB[pk],))
            act(silzfA[gp][:, jc * 256:(jc + 1) * 256], pb[pk][:, 0:256],
                AF.Silu, ins=(pB[pk],), outs=(bsil,))
            if 1 <= jc <= 7:
                ts_(decA[gp][:, (jc - 1) * 256:jc * 256],
                    silzfA[gp][:, jc * 256:(jc + 1) * 256],
                    0.0, OP.is_gt, ins=(bsil,), outs=(bdec,))
            if jc == 0:
                copy(l0f32[gp][:, :], pb[pk][:, 0:256],
                     ins=(pB[pk], bsil), outs=(W(l0f32[gp]),))

    def a_tree(gi):
        gp = gi % 2
        bsil, bdec, bsel = bsilL[gp], bdecL[gp], bselL[gp]
        for sti in range(2):
            trn(pb[6][:, 0:128], l0f32[gp][:, sti * 128:(sti + 1) * 128],
                cid_f[:, :], ins=(W(l0f32[gp]), W(cid_f)), outs=(pB[6],))
            ts_(dectm[:, 0:120], pb[6][:, 0:120], 0.0, OP.is_gt,
                ins=(pB[6],), outs=(W(dectm),))
            for (p0_, n, c0_) in ((0, 8, 8), (8, 16, 24), (24, 32, 56)):
                tt(seltm[:, c0_ + n: c0_ + 2 * n], seltm[:, p0_:p0_ + n],
                   dectm[:, p0_:p0_ + n], OP.mult,
                   ins=(W(seltm), W(dectm)), outs=(W(seltm),))
                tt(seltm[:, c0_:c0_ + n], seltm[:, p0_:p0_ + n],
                   seltm[:, c0_ + n:c0_ + 2 * n], OP.subtract,
                   ins=(W(seltm),), outs=(W(seltm),))
            tt(sel4tm[:, 64:128], seltm[:, 56:120], dectm[:, 56:120], OP.mult,
               ins=(W(seltm), W(dectm)), outs=(W(sel4tm),))
            tt(sel4tm[:, 0:64], seltm[:, 56:120], sel4tm[:, 64:128], OP.subtract,
               ins=(W(seltm), W(sel4tm)), outs=(W(sel4tm),))
            trn(pbf(6)[:, 0:128], seltm[:, :], cid_b[:, :],
                ins=(W(seltm), W(cid_b)), outs=(pB[6],))
            copy(selA[gp][:, sti * 128:(sti + 1) * 128], pbf(6)[:, 0:128],
                 ins=(pB[6],), outs=(bsel,))
            trn(pbf(7)[:, 0:128], sel4tm[:, :], cid_b[:, :],
                ins=(W(sel4tm), W(cid_b)), outs=(pB[7],))
            copy(selA[gp][:, 256 + sti * 128: 256 + (sti + 1) * 128],
                 pbf(7)[:, 0:128], ins=(pB[7],), outs=(bsel,))
        for (srcs, dst0) in (((1,), 2), ((2, 3), 4), ((4, 5, 6, 7), 8)):
            nsrc = len(srcs)
            for k, sc_ in enumerate(srcs):
                right, left = dst0 + nsrc + k, dst0 + k
                tt(selA[gp][:, right * 256:(right + 1) * 256],
                   selA[gp][:, sc_ * 256:(sc_ + 1) * 256],
                   decA[gp][:, (sc_ - 1) * 256:sc_ * 256], OP.mult,
                   ins=(bsel, bdec), outs=(bsel,))
                tt(selA[gp][:, left * 256:(left + 1) * 256],
                   selA[gp][:, sc_ * 256:(sc_ + 1) * 256],
                   selA[gp][:, right * 256:(right + 1) * 256], OP.subtract,
                   ins=(bsel,), outs=(bsel,))
        tt(silzfA[gp][:, :], silzfA[gp][:, :], selA[gp][:, :], OP.mult,
           ins=(bsil, bsel), outs=(bsil,))

    def a_wout(gi):
        gp = gi % 2
        bsil = bsilL[gp]
        for sti in range(2):
            ti = gi * 2 + sti
            pk0 = 2 + (ti % 2) * 2
            for jc in range(16):
                for sg in range(2):
                    mm(pb[pk0 + sg][:, 0:512],
                       silzfA[gp][:, jc * 256 + sti * 128:
                                   jc * 256 + (sti + 1) * 128],
                       woutf_s[:, jc * D + sg * 512: jc * D + (sg + 1) * 512],
                       jc == 0, jc == 15,
                       ins=(bsil, W(woutf_s)), outs=(pB[pk0 + sg],),
                       inc=(jc == 15 and sg == 1))
            xb = b_xta[ti % 2]
            dma("sp", xta[ti % 2][:, :], x_d[ti * 128:(ti + 1) * 128, :],
                ins=(xb,), outs=(xb,))
            ob = b_oa[ti % 2]
            act(oa[ti % 2][:, :], xta[ti % 2][:, :], AF.Copy,
                scale=stat_rs[:, ti:ti + 1], bias=stat_nmr[:, ti:ti + 1],
                ins=(xb, W(stat_rs), W(stat_nmr)), outs=(ob,))
            tt(oa[ti % 2][:, :], oa[ti % 2][:, :], xta[ti % 2][:, :], OP.add,
               ins=(ob, xb), outs=(ob,))
            for sg in range(2):
                tt(oa[ti % 2][:, sg * 512:(sg + 1) * 512],
                   oa[ti % 2][:, sg * 512:(sg + 1) * 512],
                   pb[pk0 + sg][:, 0:512], OP.add,
                   ins=(ob, pB[pk0 + sg]), outs=(ob,))
            dma("sp", part_h[ti * 128:(ti + 1) * 128, :], oa[ti % 2][:, :],
                ins=(ob,), outs=(b_part[ti],))

    for gi in range(8):
        if gi >= 1:
            a_tree(gi - 1)
        a_logits(gi)
        if gi >= 1:
            a_wout(gi - 1)
    a_tree(7)
    a_wout(7)
    dma("pool", brow_z[:, :], b_z_d, outs=(W(brow_z),))
    dma("pool", brow_o[:, :], b_o_d, outs=(W(brow_o),))

    s.barrier()

    dma("pool", woutp[0][:, :], wout_d[:, 0:8 * D], outs=(W(woutp[0]),))
    dma("pool", wzp[0][:, :],
        wz_d.rearrange("p (k j) -> p k j", k=KC)[:, :, 0:1024],
        outs=(W(wzp[0]),))
    table("natural_log_exp_and_others")
    dma("sp", sg8[:, :],
        cc_out.rearrange("(j r) t -> j r t", r=ST + 1)[:, ST, 0:8],
        ins=(b_ccout,), outs=(W(sg8),))
    act(sg8[:, :], sg8[:, :], AF.Exp, ins=(W(sg8),), outs=(W(sg8),))
    ts_(emod8[:, :], sg8[:, :], mcol_s[:, 0:1], OP.mult, minv_s[:, 0:1], OP.add,
        ins=(W(sg8), W(mcol_s), W(minv_s)), outs=(W(emod8),))
    dma("sp", em_h, emod8[:, :], ins=(W(emod8),), outs=(b_emh,))
    dma("pool", emrow_b[:, :], em_h.rearrange("j h -> () (j h)"),
        ins=(b_emh,), outs=(W(emrow_b),))
    mm(pb[7][0:ST, 0:8], cones8s[0:1, 0:ST], mrow_s[:, :], True, True,
       ins=(W(cones8s), W(mrow_s)), outs=(pB[7],))
    copy(mbc[:, :], pb[7][0:ST, 0:8], ins=(pB[7],), outs=(W(mbc),))
    memset(hglob[:, :], 0.0, outs=(W(hglob),))

    def x_hload(j):
        hb = b_hload[j % 2]
        for piece in range(4):
            dma("sp", hload[j % 2][:, piece * 512:(piece + 1) * 512],
                cc_out[j * (ST + 1): j * (ST + 1) + ST,
                       piece * 512:(piece + 1) * 512],
                ins=(hb, b_ccout), outs=(hb,))

    x_hload(0)
    dma("sp", xn2sg[0][:, :],
        xn2_h.rearrange("p (k t) -> p k t", k=KC)[:, :, 0:512],
        ins=(b_xn2h, b_xn2sg[0]), outs=(b_xn2sg[0],))

    def x_fold():
        for j in range(ncores - 1):
            hb = b_hload[j % 2]
            if j + 1 < ncores - 1:
                x_hload(j + 1)
            mm(pb[7][0:ST, 0:8], cones8s[0:1, 0:ST],
               emrow_b[:, j * 8:(j + 1) * 8],
               True, True, ins=(W(cones8s), W(emrow_b)), outs=(pB[7],))
            copy(embc[:, :], pb[7][0:ST, 0:8], ins=(pB[7],), outs=(W(embc),))
            ts_(mh_x[:, :], hload[j % 2][:, :], mbc[:, j:j + 1], OP.mult,
                ins=(hb, W(mbc)), outs=(W(mh_x),))
            for h in range(HEADS):
                stt(hglob[:, h * HD:(h + 1) * HD],
                    hglob[:, h * HD:(h + 1) * HD],
                    embc[:, h:h + 1], mh_x[:, h * HD:(h + 1) * HD],
                    OP.mult, OP.add,
                    ins=(W(hglob), W(embc), W(mh_x)), outs=(W(hglob),))
    bscC = W(bsc_tm)
    tt(scC[:, :], pg_hi[:, :], pg_lo[:, :], OP.add,
       ins=(W(pg_hi), W(pg_lo)), outs=(W(scC),))
    for c in range(NCH):
        ts_(scC[:, c * Q:(c + 1) * Q], scC[:, c * Q:(c + 1) * Q],
            slog16[:, c:c + 1], OP.add, ins=(W(scC), W(slog16)), outs=(W(scC),))
    act(scC[:, :], scC[:, :], AF.Exp, ins=(W(scC),), outs=(W(scC),))
    for c in range(NCH):
        pk = c % 2
        trn(pb[pk][:, 128:136], scC[:, c * Q:(c + 1) * Q], cid_f[0:8, 0:8],
            ins=(W(scC), W(cid_f)), outs=(pB[pk],))
        copy(bsc_tm[:, c * 8:(c + 1) * 8], pb[pk][:, 128:136],
             ins=(pB[pk],), outs=(bscC,))
    for c in range(NCH):
        mm(pb[2][0:ST, c * 8:(c + 1) * 8], cones8s[0:1, 0:ST],
           eSrow_hi[:, c * 8:(c + 1) * 8], True, False,
           ins=(W(cones8s), W(eSrow_hi)), outs=(pB[2],), inc=False)
        mm(pb[2][0:ST, c * 8:(c + 1) * 8], cones8s[0:1, 0:ST],
           eSrow_lo[:, c * 8:(c + 1) * 8], False, True,
           ins=(W(cones8s), W(eSrow_lo)), outs=(pB[2],))
    copy(eSbcA[:, :], pb[2][0:ST, 0:128], ins=(pB[2],), outs=(W(eSbcA),))
    for c in range(NCH):
        pk = c % 2
        mm(pb[pk][:, 0:128], B_fm[:, c * Q:(c + 1) * Q],
           C_fm[:, c * Q:(c + 1) * Q], True, True,
           ins=(W(B_fm), W(C_fm)), outs=(pB[pk],))
        copy(gbfA[:, c * 128:(c + 1) * 128], pb[pk][:, 0:128],
             ins=(pB[pk],), outs=(W(gbfA),))

    x_fold()

    s.barrier()

    def c_eb(p, c):
        ci = c % 2
        cb = cblk0s if p == 0 else cblk1s
        bcb = W(cb)
        sl8 = slice(c * Q, (c + 1) * Q)
        tt(ebh[ci].ap().rearrange("p (a t) -> p a t", a=4),
           cb.ap().rearrange("p (a t) -> p a t", a=4),
           pk_hi.ap()[:, sl8].unsqueeze(1).broadcast_to([8, 4, 128]),
           OP.mult, ins=(bcb, W(pk_hi)), outs=(W(ebh[ci]),))
        tt(ebl[ci].ap().rearrange("p (a t) -> p a t", a=4),
           cb.ap().rearrange("p (a t) -> p a t", a=4),
           pk_lo.ap()[:, sl8].unsqueeze(1).broadcast_to([8, 4, 128]),
           OP.mult, ins=(bcb, W(pk_lo)), outs=(W(ebl[ci]),))
        tt(ebc[ci].ap().rearrange("p (a t) -> p a t", a=4),
           cb.ap().rearrange("p (a t) -> p a t", a=4),
           ecs_bf.ap()[:, sl8].unsqueeze(1).broadcast_to([8, 4, 128]),
           OP.mult, ins=(bcb, W(ecs_bf)), outs=(W(ebc[ci]),))

    def c_front(p, c):
        pc = p * 1024
        ci = c % 2
        cb = cblk0s if p == 0 else cblk1s
        bcb = W(cb)
        sl8 = slice(c * Q, (c + 1) * Q)
        xb = b_xsc[c % 3]
        dma("sp", xsc[c % 3][:, :],
            xs_h[c * 128:(c + 1) * 128, pc:pc + 1024],
            ins=(xb, b_xsh[c]), outs=(xb,))
        ptb = b_ptc[c % 2]
        dma("sp", ptc[c % 2][:, :], part_h[c * 128:(c + 1) * 128, :],
            ins=(ptb, b_part[c]), outs=(ptb,))
        mm(pb[ci][:, 0:512], cones8s[:, :], ebh[ci][:, :], True, False,
           ins=(W(cones8s), W(ebh[ci])), outs=(pB[ci],))
        mm(pb[ci][:, 0:512], cones8s[:, :], ebl[ci][:, :], False, False,
           ins=(W(cones8s), W(ebl[ci])), outs=(pB[ci],))
        mm(pb[ci][:, 0:512], pg_hi[:, sl8], cb[:, :], False, False,
           ins=(W(pg_hi), bcb), outs=(pB[ci],))
        mm(pb[ci][:, 0:512], pg_lo[:, sl8], cb[:, :], False, False,
           ins=(W(pg_lo), bcb), outs=(pB[ci],))
        mm(pb[ci][:, 0:512], cid_b[:, :], cmnegs[:, :], False, True,
           ins=(W(cid_b), W(cmnegs)), outs=(pB[ci],))
        act(ebE[ci][:, :], pb[ci][:, 0:512], AF.Exp,
            ins=(pB[ci],), outs=(W(ebE[ci]),))
        mm(pb[1 - ci][0:ST, 0:512], cones8s[:, 0:ST], ebc[ci][:, :],
           True, True, ins=(W(cones8s), W(ebc[ci])), outs=(pB[1 - ci],))
        tt(cecs[ci].ap().rearrange("p (a t) -> p a t", a=4),
           C_fm.ap()[:, sl8].unsqueeze(1).broadcast_to([ST, 4, 128]),
           pb[1 - ci][0:ST, 0:512].rearrange("p (a t) -> p a t", a=4),
           OP.mult, ins=(W(C_fm), pB[1 - ci]), outs=(W(cecs[ci]),))

    def c_back(p, c):
        pc = p * 1024
        h0g = p * 4
        ci = c % 2
        cc = c % 8
        tt(wch[ci].ap().rearrange("p (a t) -> p a t", a=4),
           ebE[ci].ap().rearrange("p (a t) -> p a t", a=4),
           gbfA.ap()[:, c * 128:(c + 1) * 128].unsqueeze(1)
               .broadcast_to([128, 4, 128]),
           OP.mult, ins=(W(ebE[ci]), W(gbfA)), outs=(W(wch[ci]),))
        tt(wch[ci][:, :], wch[ci][:, :], idsk[:, p * 512:(p + 1) * 512],
           OP.add, ins=(W(wch[ci]), W(idsk)), outs=(W(wch[ci]),))
        copy(hglob_bf[:, pc:pc + 1024], hglob[:, pc:pc + 1024],
             ins=(W(hglob),), outs=(W(hglob_bf),))
        if c + 2 <= 15:
            c_eb(p, c + 2)
        xb = b_xsc[c % 3]
        ptb = b_ptc[c % 2]
        for hh in range(4):
            for dc in range(2):
                idx = hh * 2 + dc
                bank = 2 + idx // 4
                col = (idx % 4) * 128
                mm(pb[bank][:, col:col + 128],
                   xsc[c % 3][:, hh * HD + dc * 128:
                              hh * HD + (dc + 1) * 128],
                   wch[ci][:, hh * 128:(hh + 1) * 128],
                   True, False, ins=(xb, W(wch[ci])), outs=(pB[bank],))
                mm(pb[bank][:, col:col + 128],
                   hglob_bf[:, (h0g + hh) * HD + dc * 128:
                            (h0g + hh) * HD + (dc + 1) * 128],
                   cecs[ci][:, hh * 128:(hh + 1) * 128],
                   False, True, ins=(W(hglob_bf), W(cecs[ci])),
                   outs=(pB[bank],))
        yb = b_ygb[c % 2]
        for bk in range(2):
            tt(ygb[c % 2][:, bk * 512:(bk + 1) * 512],
               pb[2 + bk][:, 0:512],
               silz8[:, cc * 1024 + bk * 512: cc * 1024 + (bk + 1) * 512],
               OP.mult, ins=(pB[2 + bk], W(silz8)), outs=(yb,))
        tt(bsc2[ci].ap().rearrange("p (a b) -> p a b", a=4),
           B_tm.ap()[:, c * ST:(c + 1) * ST].unsqueeze(1)
               .broadcast_to([128, 4, ST]),
           bsc_tm.ap()[:, c * 8 + h0g: c * 8 + h0g + 4].unsqueeze(2)
               .broadcast_to([128, 4, ST]),
           OP.mult, ins=(W(B_tm), bscC), outs=(W(bsc2[ci]),))
        for hh in range(4):
            bank = 4 + hh // 2
            col = (hh % 2) * 256
            mm(pb[bank][0:ST, col:col + 256],
               bsc2[ci][:, hh * ST:(hh + 1) * ST],
               xsc[c % 3][:, hh * HD:(hh + 1) * HD],
               True, True, ins=(W(bsc2[ci]), xb), outs=(pB[bank],))
        for hh in range(4):
            h = h0g + hh
            bank = 4 + hh // 2
            col = (hh % 2) * 256
            stt(hglob[:, h * HD:(h + 1) * HD],
                hglob[:, h * HD:(h + 1) * HD],
                eSbcA[:, c * 8 + h:c * 8 + h + 1],
                pb[bank][0:ST, col:col + 256], OP.mult, OP.add,
                ins=(W(hglob), W(eSbcA), pB[bank]), outs=(W(hglob),))
        for jc in range(8):
            for sgm in range(2):
                mm(pb[6 + sgm][:, 0:512],
                   ygb[c % 2][:, jc * 128:(jc + 1) * 128],
                   woutp[p][:, jc * D + sgm * 512: jc * D + (sgm + 1) * 512],
                   jc == 0, jc == 7 and p == 1,
                   ins=(yb, W(woutp[p])), outs=(pB[6 + sgm],),
                   inc=(jc == 7 and p == 1))
        if p == 0:
            mm(pb[6][:, 0:512], conesrs[0:1, 0:128], brow_o[:, 0:512],
               False, True, ins=(W(conesrs), W(brow_o)), outs=(pB[6],))
            mm(pb[7][:, 0:512], conesrs[0:1, 0:128], brow_o[:, 512:1024],
               False, True, ins=(W(conesrs), W(brow_o)), outs=(pB[7],))
        for sgm in range(2):
            tt(ptc[c % 2][:, sgm * 512:(sgm + 1) * 512],
               ptc[c % 2][:, sgm * 512:(sgm + 1) * 512],
               pb[6 + sgm][:, 0:512], OP.add,
               ins=(ptb, pB[6 + sgm]), outs=(ptb,))
        if p == 0:
            dma("sp", part_h[c * 128:(c + 1) * 128, :], ptc[c % 2][:, :],
                ins=(ptb,), outs=(b_part[c],))
        else:
            dma("sp", y_d[c * 128:(c + 1) * 128, :], ptc[c % 2][:, :],
                ins=(ptb,), outs=(b_y,))

    prev = None
    for p in range(2):
        pc = p * 1024
        if p == 0:
            dma("pool", wzp[1][:, :],
                wz_d.rearrange("p (k j) -> p k j", k=KC)[:, :, 1024:2048],
                outs=(W(wzp[1]),))
            dma("pool", woutp[1][:, :], wout_d[:, 8 * D:16 * D],
                outs=(W(woutp[1]),))
        c_eb(p, 0)
        c_eb(p, 1)
        for h8 in range(2):
            cbase = h8 * 8
            if prev is not None:
                c_back(*prev)
                prev = None
            table("silu_and_others")
            for sgi in range(2):
                t0 = (cbase + sgi * 4) * Q
                sgb = b_xn2sg[(cbase // 4 + sgi) % 2]
                for jc in range(8):
                    pk = jc % 2
                    for kc in range(KC):
                        mm(pb[pk][:, 0:512],
                           wzp[p][:, kc * 1024 + jc * 128:
                                  kc * 1024 + (jc + 1) * 128],
                           xn2sg[(cbase // 4 + sgi) % 2][:, kc * 512:(kc + 1) * 512],
                           kc == 0, False, ins=(W(wzp[p]), sgb), outs=(pB[pk],))
                    mm(pb[pk][:, 0:512], brow_z[:, pc + jc * 128: pc + (jc + 1) * 128],
                       conesrs[:, :], False, True,
                       ins=(W(brow_z), W(conesrs)), outs=(pB[pk],))
                    act(silz8.ap().rearrange("p (a b t) -> p a b t", a=8, b=8)
                        [:, sgi * 4:(sgi + 1) * 4, jc, :],
                        pb[pk][:, 0:512].rearrange("p (a t) -> p a t", a=4),
                        AF.Silu, ins=(pB[pk],), outs=(W(silz8),))
                nxt = cbase // 4 + sgi + 1
                if nxt < 4 or (p == 0 and nxt == 4):
                    t1 = (nxt % 4) * 512
                    nb = b_xn2sg[nxt % 2]
                    dma("sp", xn2sg[nxt % 2][:, :],
                        xn2_h.rearrange("p (k t) -> p k t", k=KC)
                             [:, :, t1:t1 + 512],
                        ins=(b_xn2h, nb), outs=(nb,))
            table("natural_log_exp_and_others")
            for cc in range(8):
                c = cbase + cc
                c_front(p, c)
                if prev is not None:
                    c_back(*prev)
                prev = (p, c)
    c_back(*prev)

    finals = [(k, s.cnt[k]) for k in DMAKEYS if k.startswith("dma_sp") and s.cnt[k]]

    def th_fin(engine):
        for k, v in finals:
            engine.wait_ge(s.sems[k], v)
    s.thunks["sp"].append(th_fin)

    from contextlib import ExitStack
    with ExitStack() as stack:
        s.sems = {}
        for k in SEMKEYS:
            s.sems[k] = stack.enter_context(nc.semaphore("s_" + k))
        block = stack.enter_context(nc.Block())

        @block.tensor
        def _(eng):
            for th in s.thunks["pe"]:
                th(eng)

        @block.scalar
        def _(eng):
            for th in s.thunks["act"]:
                th(eng)

        @block.vector
        def _(eng):
            for th in s.thunks["dve"]:
                th(eng)

        @block.gpsimd
        def _(eng):
            for th in s.thunks["pool"]:
                th(eng)

        @block.sync
        def _(eng):
            for th in s.thunks["sp"]:
                th(eng)

    return nc


def kernel(**inputs):
    maps = prep_inputs(inputs)
    nc = build(NCORES)
    res = run_bass_kernel_spmd(nc, maps, core_ids=list(range(NCORES)))
    out = np.concatenate([np.asarray(r["y"]) for r in res.results], axis=0)
    return out.reshape(1, 16384, D).astype(np.float32)


# version 17 (final)
# speedup vs baseline: 1.1652x; 1.1652x over previous
import numpy as np
import ml_dtypes
import concourse.bass as bass
import concourse.mybir as mybir
from concourse.bass_utils import run_bass_kernel_spmd

F32 = mybir.dt.float32
BF16 = mybir.dt.bfloat16
AF = mybir.ActivationFunctionType
OP = mybir.AluOpType
NPBF = ml_dtypes.bfloat16

NCORES = 8
TOK = 2048
Q = 128
NCH = TOK // Q
D = 1024
KC = D // 128
INNER = 2048
HEADS = 8
HD = 256
ST = 64
PROJ_TM = INNER + ST
NJ = 2048
EPS = 1e-5
NEG = -1.0e30

ENGS = ("pe", "act", "dve", "pool", "sp")
NDL = 8
DMAKEYS = tuple(f"dma_{q}{l}" for q in ("sp", "pool") for l in range(NDL))
SEMKEYS = ENGS + DMAKEYS + ("cc",)


class Buf:
    __slots__ = ("name", "w", "readers")

    def __init__(self, name=""):
        self.name = name
        self.w = None
        self.readers = {}


class Sch:
    def __init__(self):
        self.thunks = {e: [] for e in ENGS}
        self.cnt = {k: 0 for k in SEMKEYS}
        self.dix = {"sp": 0, "pool": 0}
        self.waited = {}
        self.sems = {}

    def _wait_list(self, eng, ins, outs, dma, mykey=None):
        if mykey is None:
            mykey = eng
        need = {}

        def add(sk, v):
            if v is None:
                return
            if (not dma) and sk == eng == "pe":
                return
            if need.get(sk, 0) < v:
                need[sk] = v

        for b in ins:
            if b.w:
                add(*b.w)
        for b in outs:
            if b.w:
                add(*b.w)
            for sk, v in b.readers.items():
                add(sk, v)
        waits = []
        for sk, v in need.items():
            if self.waited.get((eng, sk), 0) < v:
                waits.append((sk, v))
                self.waited[(eng, sk)] = v
        return waits

    def op(self, eng, build, ins=(), outs=(), inc=True, dma=False):
        if dma:
            lane = self.dix[eng] % NDL
            self.dix[eng] += 1
            mykey = f"dma_{eng}{lane}"
        else:
            mykey = eng
        waits = self._wait_list(eng, ins, outs, dma, mykey=mykey)
        if dma:
            prior = self.cnt[mykey]
            if prior > 0 and self.waited.get((eng, mykey), 0) < prior:
                waits.append((mykey, prior))
                self.waited[(eng, mykey)] = prior
            self.cnt[mykey] += 16
            myv, step = self.cnt[mykey], 16
        elif inc:
            self.cnt[mykey] += 1
            myv, step = self.cnt[mykey], 1
        else:
            myv, step = self.cnt[mykey] + 1, 0
        for b in outs:
            b.w = (mykey, myv)
        for b in ins:
            if b.readers.get(mykey, 0) < myv:
                b.readers[mykey] = myv
        sch = self

        def th(engine, waits=tuple(waits), build=build, step=step, mykey=mykey):
            for sk, v in waits:
                engine.wait_ge(sch.sems[sk], v)
            inst = build(engine)
            if step:
                inst.then_inc(sch.sems[mykey], step)

        self.thunks[eng].append(th)

    def barrier(self):
        sch = self
        for e in ENGS:
            pre = self.cnt[e]
            self.cnt[e] += 1

            def th(engine, e=e, pre=pre):
                if pre:
                    engine.wait_ge(sch.sems[e], pre)
                engine.nop().then_inc(sch.sems[e], 1)
            self.thunks[e].append(th)
        snap = dict(self.cnt)
        snap.pop("cc", None)
        self.cnt["dve"] += 1
        bval = self.cnt["dve"]

        pre_dve = snap["dve"]

        def th_dve(engine, snap=dict(snap), pre_dve=pre_dve):
            for sk, v in snap.items():
                if sk != "dve" and v > 0:
                    engine.wait_ge(sch.sems[sk], v)
            engine.wait_ge(sch.sems["dve"], pre_dve)
            engine.nop().then_inc(sch.sems["dve"], 1)
        self.thunks["dve"].append(th_dve)
        for e in ENGS:
            if e == "dve":
                continue

            def th2(engine, bval=bval):
                engine.wait_ge(sch.sems["dve"], bval)
            self.thunks[e].append(th2)
        for e in ENGS:
            for sk, v in snap.items():
                if self.waited.get((e, sk), 0) < v:
                    self.waited[(e, sk)] = v
            if self.waited.get((e, "dve"), 0) < bval:
                self.waited[(e, "dve")] = bval


class Arena:

    def __init__(self, nc, budget_bytes=204 * 1024):
        self.nc = nc
        self.budget = budget_bytes
        self.items = []
        self.handles = {}

    def alloc(self, name, shape, dtype, p0=0, p1=5):
        per = int(np.prod(shape[1:])) * mybir.dt.size(dtype)
        per = (per + 31) // 32 * 32
        cands = sorted(
            (o, sz) for (o, sz, a, b) in self.items if not (b < p0 or a > p1)
        )
        gaps = []
        cur = 0
        for o, sz in cands:
            if o > cur:
                gaps.append((cur, o - cur))
            cur = max(cur, o + sz)
        gaps.append((cur, self.budget - cur))
        fit = sorted((g for g in gaps if g[1] >= per), key=lambda g: g[1])
        assert fit, (
            f"SBUF overflow: {name} {per}B (gaps {gaps[-1]}) > {self.budget}")
        off = fit[0][0]
        self.items.append((off, per, p0, p1))
        h = self.nc.virtual_sbuf_tensor(name, list(shape), dtype)
        self.handles[name] = (h, off)
        return h

    def finalize(self):
        size = max(o + sz for (o, sz, _, _) in self.items)
        big = self.nc.alloc_sbuf_tensor("arena", [128, size], mybir.dt.uint8)
        base = self.nc.lookup_mloc(big).addr
        for name, (h, off) in self.handles.items():
            self.nc._tensor(h.name, h.shape, h.dtype, type="SB")
            mloc = self.nc.lookup_mloc(h)
            mloc.allocated = True
            mloc.addr = base + off
        return size


def _ffn_perm():
    lvl_off = [0, 8, 24, 56, 128, 256, 512, 1024]
    perm = np.full(NJ, -1, np.int64)
    for d in range(8):
        for r in range(8 * (1 << d)):
            tree, bits = r % 8, r // 8
            pos = 0
            for l in range(1, d + 1):
                pos += ((bits >> (l - 1)) & 1) << (d - l)
            perm[lvl_off[d] + r] = tree * 255 + (1 << d) - 1 + pos
    return perm


def _kc_layout(w, cols):
    return np.ascontiguousarray(
        w.reshape(KC, 128, cols).transpose(1, 0, 2).reshape(128, KC * cols))


def _rows_layout(w, rows, cols):
    n = rows // 128
    return np.ascontiguousarray(
        w.reshape(n, 128, cols).transpose(1, 0, 2).reshape(128, n * cols))


def prep_inputs(inputs):
    x = np.asarray(inputs["L0_attn_input_tensor"], np.float32).reshape(16384, D)
    W_in = np.asarray(inputs["L0_attn_W_in"], np.float32)
    b_in = np.asarray(inputs["L0_attn_b_in"], np.float32)
    W_out = np.asarray(inputs["L0_attn_W_out"], np.float32)
    b_out = np.asarray(inputs["L0_attn_b_out"], np.float32)
    Wf_in = np.asarray(inputs["L0_ffn_W_in"], np.float32)
    bf_in = np.asarray(inputs["L0_ffn_b_in"], np.float32)
    Wf_out = np.asarray(inputs["L0_ffn_W_out"], np.float32)

    perm = _ffn_perm()
    ok = perm >= 0
    Winf_p = np.zeros((NJ, D), np.float32)
    Winf_p[ok] = Wf_in[perm[ok]]
    binf_p = np.zeros(NJ, np.float32)
    binf_p[ok] = bf_in[perm[ok]]
    Woutf_p = np.zeros((NJ, D), np.float32)
    Woutf_p[ok] = Wf_out[:, perm[ok]].T

    com = {
        "wproj": _kc_layout(np.hstack([W_in[:, INNER:2 * INNER],
                                       W_in[:, 2 * INNER:2 * INNER + ST]]),
                            PROJ_TM),
        "wbc": _kc_layout(W_in[:, 2 * INNER:2 * INNER + 2 * ST], 2 * ST),
        "wdtf": _kc_layout(np.hstack([W_in[:, 2 * INNER + 2 * ST:],
                                      np.zeros((D, 24), np.float32)]), 32),
        "wz": _kc_layout(W_in[:, :INNER], INNER),
        "wout": _rows_layout(W_out, INNER, D),
        "winf": _kc_layout(np.ascontiguousarray(Winf_p.T), NJ),
        "woutf": _rows_layout(Woutf_p, NJ, D),
        "b_xsb": np.hstack([b_in[INNER:2 * INNER],
                            b_in[2 * INNER:2 * INNER + ST]]).reshape(1, -1),
        "b_bc": b_in[2 * INNER:2 * INNER + 2 * ST].reshape(1, -1).copy(),
        "b_dt": np.hstack([b_in[2 * INNER + 2 * ST:],
                           np.zeros(24, np.float32)]).reshape(1, -1),
        "dtb": np.hstack([np.asarray(inputs["L0_attn_dt_bias"], np.float32),
                          np.zeros(24, np.float32)]).reshape(1, -1),
        "b_z": b_in[:INNER].reshape(1, -1).copy(),
        "b_o": b_out.reshape(1, -1).copy(),
        "b_inf": binf_p.reshape(1, -1),
        "gam": np.asarray(inputs["L0_attn_ln_gamma"], np.float32).reshape(1, D),
        "bet": np.asarray(inputs["L0_attn_ln_beta"], np.float32).reshape(1, D),
        "alog": np.asarray(inputs["L0_attn_A_log"], np.float32).reshape(1, -1),
        "dsk": np.asarray(inputs["L0_attn_D_skip"], np.float32).reshape(1, -1),
        "cid_bf": np.eye(128, dtype=np.float32).astype(NPBF),
        "cid_f": np.eye(128, dtype=np.float32),
        "cones8": np.ones((8, 128), np.float32).astype(NPBF),
        "constf": np.tile(np.array([EPS, 1.0, -1.0, -0.5, 0.0,
                                    1.0 / D, 40.0, 2.0], np.float32),
                          (128, 1)),
        "conesr": np.ones((1, 512), np.float32).astype(NPBF),
    }
    cb0 = np.zeros((8, 512), np.float32)
    cb1 = np.zeros((8, 512), np.float32)
    for hh in range(4):
        cb0[hh, hh * 128:(hh + 1) * 128] = 1.0
        cb1[hh + 4, hh * 128:(hh + 1) * 128] = 1.0
    com["cblk0"] = cb0.astype(NPBF)
    com["cblk1"] = cb1.astype(NPBF)
    mneg = np.zeros((128, 512), np.float32)
    tri = np.tril(np.full((Q, Q), NEG, np.float32), -1)
    for hh in range(4):
        mneg[:, hh * 128:(hh + 1) * 128] = tri
    com["cmneg"] = mneg.astype(NPBF)

    maps = []
    for c in range(NCORES):
        m = np.zeros(8, np.float32)
        m[:c] = 1.0
        mp = dict(com)
        mp["x"] = np.ascontiguousarray(x[c * TOK:(c + 1) * TOK])
        mp["m_col"] = m.reshape(8, 1).copy()
        mp["minv_col"] = (1.0 - m).reshape(8, 1).copy()
        mp["m_row"] = m.reshape(1, 8).astype(NPBF)
        maps.append(mp)
    return maps


def build(ncores=NCORES):
    nc = bass.Bass("TRN2", target_bir_lowering=False, debug=False)
    s = Sch()

    def din(name, shape, dtype=F32):
        return nc.dram_tensor(name, list(shape), dtype, kind="ExternalInput").ap()

    x_d = din("x", (TOK, D))
    wproj_d = din("wproj", (128, KC * PROJ_TM))
    wbc_d = din("wbc", (128, KC * 2 * ST))
    wdtf_d = din("wdtf", (128, KC * 32))
    wz_d = din("wz", (128, KC * INNER))
    wout_d = din("wout", (128, 16 * D))
    winf_d = din("winf", (128, KC * NJ))
    woutf_d = din("woutf", (128, 16 * D))
    b_xsb_d = din("b_xsb", (1, PROJ_TM))
    b_bc_d = din("b_bc", (1, 2 * ST))
    b_dt_d = din("b_dt", (1, 32))
    dtb_d = din("dtb", (1, 32))
    b_z_d = din("b_z", (1, INNER))
    b_o_d = din("b_o", (1, D))
    b_inf_d = din("b_inf", (1, NJ))
    gam_d = din("gam", (1, D))
    bet_d = din("bet", (1, D))
    alog_d = din("alog", (1, HEADS))
    dsk_d = din("dsk", (1, HEADS))
    cid_bf_d = din("cid_bf", (128, 128), BF16)
    cid_f_d = din("cid_f", (128, 128))
    cones8_d = din("cones8", (8, 128), BF16)
    constf_d = din("constf", (128, 8))
    conesr_d = din("conesr", (1, 512), BF16)
    cblk0_d = din("cblk0", (8, 512), BF16)
    cblk1_d = din("cblk1", (8, 512), BF16)
    cmneg_d = din("cmneg", (128, 512), BF16)
    mcol_d = din("m_col", (8, 1))
    minv_d = din("minv_col", (8, 1))
    mrow_d = din("m_row", (1, 8), BF16)

    y_d = nc.dram_tensor("y", [TOK, D], F32, kind="ExternalOutput").ap()
    xs_h = nc.dram_tensor("xs_h", [TOK, INNER], BF16).ap()
    xn_h = nc.dram_tensor("xn_h", [128, KC * TOK], BF16).ap()
    xn2_h = nc.dram_tensor("xn2_h", [128, KC * TOK], BF16).ap()
    part_h = nc.dram_tensor("part_h", [TOK, D], F32).ap()
    em_h = nc.dram_tensor("em_h", [8, 8], F32).ap()
    es_h = nc.dram_tensor("es_h", [16, 8], BF16).ap()
    cc_in = nc.dram_tensor("cc_in", [ST + 1, TOK], F32).ap()
    cc_out = nc.dram_tensor("cc_out", [(ST + 1) * ncores, TOK], F32,
                            addr_space="Shared").ap()

    ar = Arena(nc)
    A = ar.alloc
    cid_b = A("cid_b", [128, 128], BF16, 0, 5)
    cid_f = A("cid_f", [128, 128], F32, 0, 5)
    cones8s = A("cones8s", [8, 128], BF16, 0, 5)
    constf = A("constf", [128, 8], F32, 0, 5)
    conesrs = A("conesrs", [1, 512], BF16, 0, 5)
    cblk0s = A("cblk0s", [8, 512], BF16, 0, 5)
    cblk1s = A("cblk1s", [8, 512], BF16, 0, 5)
    cmnegs = A("cmnegs", [128, 512], BF16, 0, 5)
    idsk = A("idsk", [128, 1024], BF16, 0, 5)
    acol = A("acol", [8, 1], F32, 0, 1)
    stat_rs = A("stat_rs", [128, 16], F32, 0, 2)
    stat_nmr = A("stat_nmr", [128, 16], F32, 0, 2)
    mcol_s = A("mcol_s", [8, 1], F32, 0, 3)
    minv_s = A("minv_s", [8, 1], F32, 0, 3)
    mrow_s = A("mrow_s", [1, 8], BF16, 0, 3)
    xn2_bf = A("xn2_bf", [128, KC * TOK], BF16, 0, 1)
    pk_hi = A("pk_hi", [8, TOK], BF16, 1, 5)
    pk_lo = A("pk_lo", [8, TOK], BF16, 1, 5)
    pg_hi = A("pg_hi", [8, TOK], BF16, 1, 5)
    pg_lo = A("pg_lo", [8, TOK], BF16, 1, 5)
    ecs_bf = A("ecs_bf", [8, TOK], BF16, 1, 5)
    B_tm = A("B_tm", [128, NCH * ST], BF16, 1, 5)
    B_fm = A("B_fm", [64, NCH * Q], BF16, 1, 5)
    C_fm = A("C_fm", [64, NCH * Q], BF16, 1, 5)
    eSrow_hi = A("eSrow_hi", [1, 128], BF16, 1, 5)
    eSrow_lo = A("eSrow_lo", [1, 128], BF16, 1, 5)
    slog16 = A("slog16", [8, 16], F32, 1, 5)
    scq = A("scq", [8, 128], F32, 1, 5)
    tmq = A("tmq", [128, 16], F32, 1, 5)
    bsc_all = [A(f"bsc_all{i}", [128, 8 * ST], BF16, 1, 1)
               for i in range(2)]
    wproj_s = A("wproj_s", [128, KC * PROJ_TM], BF16, 0, 1)
    wbc_s = A("wbc_s", [128, KC * 2 * ST], BF16, 0, 1)
    wdtf_s = A("wdtf_s", [128, KC * 32], BF16, 0, 1)
    winf_s = A("winf_s", [128, KC * NJ], BF16, 0, 2)
    brow_xsb = A("brow_xsb", [1, PROJ_TM], BF16, 0, 1)
    brow_bc = A("brow_bc", [1, 2 * ST], BF16, 0, 1)
    brow_dt = A("brow_dt", [1, 32], BF16, 0, 1)
    brow_dtb = A("brow_dtb", [1, 32], BF16, 0, 1)
    brow_inf = A("brow_inf", [1, NJ], BF16, 0, 2)
    gam_s = A("gam_s", [128, D], F32, 0, 0)
    bet_s = A("bet_s", [128, D], F32, 0, 0)
    dbc = A("dbc", [128, 8], F32, 0, 0)
    dskb = A("dskb", [1, 8], BF16, 0, 0)
    NP0 = 4
    xt = [A(f"xt{i}", [128, D], F32, 0, 0) for i in range(NP0)]
    ln_a = [A(f"ln_a{i}", [128, D], F32, 0, 0) for i in range(NP0)]
    ln_b = [A(f"ln_b{i}", [128, D], F32, 0, 0) for i in range(NP0)]
    ln_c = [A(f"ln_c{i}", [128, D], F32, 0, 0) for i in range(NP0)]
    ln_st = [A(f"ln_st{i}", [128, 8], F32, 0, 0) for i in range(NP0)]
    xnst = [A(f"xnst{i}", [128, D], BF16, 0, 0) for i in range(NP0)]
    sA = A("sA", [8, TOK], F32, 1, 1)
    sB = A("sB", [8, TOK], F32, 1, 1)
    sC = A("sC", [8, TOK], F32, 1, 1)
    cs = A("cs", [8, 1 + TOK], F32, 1, 1)
    esc16 = A("esc16", [8, 32], F32, 1, 1)
    esc16b = A("esc16b", [16, 8], BF16, 1, 1)
    xsb = [A(f"xsb{i}", [128, INNER], BF16, 1, 1) for i in range(3)]
    bsc_tm = A("bsc_tm", [128, NCH * 8], F32, 1, 5)
    hloc_sb = A("hloc_sb", [ST, HEADS * HD], F32, 1, 1)
    xn_bf = A("xn_bf", [128, KC * TOK], BF16, 2, 2)
    woutf_s = A("woutf_s", [128, 16 * D], BF16, 2, 2)
    silzfA = [A(f"silzfA{i}", [128, 16 * 256], BF16, 2, 2) for i in range(2)]
    decA = [A(f"decA{i}", [128, 7 * 256], BF16, 2, 2) for i in range(2)]
    selA = [A(f"selA{i}", [128, 16 * 256], BF16, 2, 2) for i in range(2)]
    l0f32 = [A(f"l0f32{i}", [128, 256], F32, 2, 2)
             for i in range(2)]
    dectm = A("dectm", [128, 120], BF16, 2, 2)
    seltm = A("seltm", [128, 128], BF16, 2, 2)
    sel4tm = A("sel4tm", [128, 128], BF16, 2, 2)
    oa = [A(f"oa{i}", [128, D], F32, 2, 2) for i in range(2)]
    xta = [A(f"xta{i}", [128, D], F32, 2, 2) for i in range(2)]
    hglob = A("hglob", [ST, HEADS * HD], F32, 3, 5)
    hglob_bf = A("hglob_bf", [ST, HEADS * HD], BF16, 3, 5)
    hload = [A(f"hload{i}", [ST, HEADS * HD], F32, 3, 3) for i in range(2)]
    mh_x = A("mh_x", [ST, HEADS * HD], F32, 3, 3)
    sg8 = A("sg8", [8, 8], F32, 3, 3)
    emod8 = A("emod8", [8, 8], F32, 3, 3)
    emrow_b = A("emrow_b", [1, 64], BF16, 3, 3)
    embc = A("embc", [ST, 8], F32, 3, 3)
    mbc = A("mbc", [ST, 8], F32, 3, 3)
    wzp = [A("wzp0", [128, KC * 1024], BF16, 3, 4),
           A("wzp1", [128, KC * 1024], BF16, 4, 5)]
    woutp = [A("woutp0", [128, 8 * D], BF16, 3, 4),
             A("woutp1", [128, 8 * D], BF16, 4, 5)]
    brow_z = A("brow_z", [1, INNER], BF16, 2, 5)
    brow_o = A("brow_o", [1, D], BF16, 2, 5)
    xn2sg = [A(f"xn2sg{i}", [128, KC * 512], BF16, 3, 5) for i in range(2)]
    silz8 = A("silz8", [128, 8 * 1024], BF16, 3, 5)
    xsc = [A(f"xsc{i}", [128, 1024], BF16, 4, 5) for i in range(3)]
    ptc = [A(f"ptc{i}", [128, D], F32, 4, 5) for i in range(2)]
    ebh = [A(f"ebh{i}", [8, 512], BF16, 4, 5) for i in range(2)]
    ebl = [A(f"ebl{i}", [8, 512], BF16, 4, 5) for i in range(2)]
    ebc = [A(f"ebc{i}", [8, 512], BF16, 4, 5) for i in range(2)]
    ebE = [A(f"ebE{i}", [128, 512], BF16, 4, 5) for i in range(2)]
    wch = [A(f"wch{i}", [128, 512], BF16, 4, 5) for i in range(2)]
    cecs = [A(f"cecs{i}", [ST, 512], BF16, 4, 5) for i in range(2)]
    bsc2 = [A(f"bsc2{i}", [128, 4 * ST], BF16, 4, 5) for i in range(2)]
    gbfA = A("gbfA", [128, NCH * 128], BF16, 3, 5)
    eSbcA = A("eSbcA", [ST, NCH * 8], F32, 3, 5)
    scC = A("scC", [8, TOK], F32, 3, 3)
    ygb = [A(f"ygb{i}", [128, 8 * Q], BF16, 4, 5) for i in range(2)]

    ar.finalize()

    pb = [nc.alloc_psum_tensor(f"pb{k}", [128, 512], F32) for k in range(8)]
    pB = [Buf(f"pb{k}") for k in range(8)]

    bufs = {}

    def W(h):
        if h.name not in bufs:
            bufs[h.name] = Buf(h.name)
        return bufs[h.name]

    b_xsb = [Buf(f"xsb{i}") for i in range(3)]
    b_xsc = [Buf(f"xsc{i}") for i in range(3)]
    b_xt = [Buf(f"xt{i}") for i in range(4)]
    b_xnst = [Buf(f"xnst{i}") for i in range(4)]
    b_xta = [Buf(f"xta{i}") for i in range(2)]
    b_ptc = [Buf(f"ptc{i}") for i in range(2)]
    b_oa = [Buf(f"oa{i}") for i in range(2)]
    b_ygb = [Buf(f"ygb{i}") for i in range(2)]
    b_hload = [Buf(f"hload{i}") for i in range(2)]
    b_xn2sg = [Buf(f"xn2sg{i}") for i in range(2)]
    b_y = Buf("y_dram")
    b_part = [Buf(f"part{i}") for i in range(16)]
    b_xsh = [Buf(f"xs_h{c}") for c in range(16)]
    b_xnh = Buf("xn_h")
    b_xn2h = Buf("xn2_h")
    b_ccin = Buf("cc_in")
    b_ccout = Buf("cc_out")
    b_emh = Buf("em_h")
    b_esh = Buf("es_h")

    def dma(q, out, in_, ins=(), outs=()):
        s.op(q, lambda e, o=out, i=in_: e.dma_start(out=o, in_=i),
             ins=ins, outs=outs, dma=True)

    def mm(out, lhsT, rhs, start, stop, ins=(), outs=(), inc=None):
        s.op("pe",
             lambda e, o=out, l=lhsT, r=rhs, a=start, b=stop:
                 e.matmul(o, l, r, start=a, stop=b),
             ins=ins, outs=outs, inc=(stop if inc is None else inc))

    def act(out, in_, func, bias=None, scale=None, ins=(), outs=()):
        kw = {}
        if bias is not None:
            kw["bias"] = bias
        if scale is not None:
            kw["scale"] = scale
        if kw and func == AF.Copy:
            func = AF.Identity
        s.op("act",
             lambda e, o=out, i=in_, f=func, k=dict(kw): e.activation(o, i, f, **k),
             ins=ins, outs=outs)

    def tt(out, a, b, op_, ins=(), outs=()):
        s.op("dve", lambda e, o=out, p=a, q=b, r=op_: e.tensor_tensor(o, p, q, r),
             ins=ins, outs=outs)

    def ts_(out, in_, s1, op0, s2=None, op1=None, ins=(), outs=()):
        def bld(e, o=out, i=in_, a=s1, b=s2, p0=op0, p1=op1):
            if p1 is None:
                return e.tensor_scalar(o, i, a, None, p0)
            return e.tensor_scalar(o, i, a, b, p0, p1)
        s.op("dve", bld, ins=ins, outs=outs)

    def stt(out, in0, sc, in1, op0, op1, ins=(), outs=()):
        s.op("dve",
             lambda e, o=out, i0=in0, c=sc, i1=in1, p0=op0, p1=op1:
                 e.scalar_tensor_tensor(o, i0, c, i1, p0, p1),
             ins=ins, outs=outs)

    def ttr(out, in0, in1, accum, ins=(), outs=()):
        s.op("dve",
             lambda e, o=out, a=in0, b=in1, ac=accum:
                 e.tensor_tensor_reduce(o, a, b, 1.0, 0.0, OP.mult, OP.add, ac),
             ins=ins, outs=outs)

    def copy(out, in_, ins=(), outs=()):
        s.op("dve", lambda e, o=out, i=in_: e.tensor_copy(o, i),
             ins=ins, outs=outs)

    def trn(outp, in_, ident, ins=(), outs=()):
        s.op("pe", lambda e, o=outp, i=in_, d=ident: e.transpose(o, i, d),
             ins=ins, outs=outs)

    def red(out, in_, ins=(), outs=()):
        s.op("dve",
             lambda e, o=out, i=in_:
                 e.tensor_reduce(o, i, mybir.AxisListType.X, OP.add),
             ins=ins, outs=outs)

    def memset(buf_ap, val, ins=(), outs=()):
        s.op("dve", lambda e, o=buf_ap, v=val: e.memset(o, v),
             ins=ins, outs=outs)

    from concourse.hw_specs import get_activation_tables
    _tab_names = list(get_activation_tables(nc.m.arch))

    def table(name):
        idx = _tab_names.index(name)

        def bld(e, idx=idx):
            return e.add_instruction(
                mybir.InstLoadActFuncSet(
                    name=nc.get_next_instruction_name(),
                    act_func_set_id=idx, ins=[], outs=[]))
        s.op("act", bld, inc=False)

    def pbf(k):
        return pb[k].ap().bitcast(BF16)

    dma("sp", cid_b[:, :], cid_bf_d, outs=(W(cid_b),))
    dma("sp", cid_f[:, :], cid_f_d, outs=(W(cid_f),))
    dma("sp", cones8s[:, :], cones8_d, outs=(W(cones8s),))
    dma("sp", constf[:, :], constf_d, outs=(W(constf),))
    for _ci, _cv in enumerate([EPS, 1.0, -1.0, -0.5, 0.0, 1.0 / D, 40.0, 2.0]):
        nc.const_aps.register(F32, _cv, constf[:, _ci:_ci + 1])
    dma("sp", conesrs[:, :], conesr_d, outs=(W(conesrs),))
    dma("sp", cblk0s[:, :], cblk0_d, outs=(W(cblk0s),))
    dma("sp", cblk1s[:, :], cblk1_d, outs=(W(cblk1s),))
    dma("sp", cmnegs[:, :], cmneg_d, outs=(W(cmnegs),))
    dma("sp", gam_s[:, :],
        bass.AP(gam_d.tensor, 0, [[0, 128], [1, D]]), outs=(W(gam_s),))
    dma("sp", bet_s[:, :],
        bass.AP(bet_d.tensor, 0, [[0, 128], [1, D]]), outs=(W(bet_s),))
    dma("sp", mcol_s[:, :], mcol_d, outs=(W(mcol_s),))
    dma("sp", minv_s[:, :], minv_d, outs=(W(minv_s),))
    dma("sp", mrow_s[:, :], mrow_d, outs=(W(mrow_s),))
    dma("sp", acol[:, :], alog_d.rearrange("o h -> h o"), outs=(W(acol),))
    table("natural_log_exp_and_others")
    act(acol[:, :], acol[:, :], AF.Exp, ins=(W(acol), W(constf)), outs=(W(acol),))
    ts_(acol[:, :], acol[:, :], -1.0, OP.mult, ins=(W(acol),), outs=(W(acol),))
    dma("pool", brow_xsb[:, :], b_xsb_d, outs=(W(brow_xsb),))
    dma("pool", brow_bc[:, :], b_bc_d, outs=(W(brow_bc),))
    dma("pool", brow_dt[:, :], b_dt_d, outs=(W(brow_dt),))
    dma("pool", brow_dtb[:, :], dtb_d, outs=(W(brow_dtb),))
    dma("pool", brow_inf[:, :], b_inf_d, outs=(W(brow_inf),))
    dma("pool", dskb[:, :], dsk_d, outs=(W(dskb),))
    mm(pb[7][:, 0:8], conesrs[0:1, 0:128], dskb[:, :], True, True,
       ins=(W(conesrs), W(dskb)), outs=(pB[7],))
    copy(dbc[:, :], pb[7][:, 0:8], ins=(pB[7], W(constf)), outs=(W(dbc),))
    tt(idsk.ap().rearrange("p (a b) -> p a b", a=8),
       cid_f.ap().unsqueeze(1).broadcast_to([128, 8, 128]),
       dbc.ap().unsqueeze(2).broadcast_to([128, 8, 128]),
       OP.mult, ins=(W(cid_f), W(dbc)), outs=(W(idsk),))

    b_lna = [Buf(f"lna{i}") for i in range(4)]
    b_lnb = [Buf(f"lnb{i}") for i in range(4)]
    b_lnc = [Buf(f"lnc{i}") for i in range(4)]
    b_lst = [Buf(f"lst{i}") for i in range(4)]

    def p0_stage(fn, tiles):
        for i in tiles:
            par = i % 4
            fn(i, par, xt[par], ln_a[par], ln_b[par], ln_c[par], ln_st[par],
               b_xt[par], b_lna[par], b_lnb[par], b_lnc[par], b_lst[par])

    def p0_dma(i, par, x, la, lb, lc, st_, xb, ba, bb_, bc_, bst):
        dma("sp", x[:, :], x_d[i * 128:(i + 1) * 128, :],
            ins=(xb,), outs=(xb,))

    p0_stages = [
        lambda i, par, x, la, lb, lc, st_, xb, ba, bb_, bc_, bst: (
            red(st_[:, 0:1], x[:, :], ins=(xb,), outs=(bst,)),
            ts_(st_[:, 1:2], st_[:, 0:1], 1.0 / D, OP.mult,
                ins=(bst,), outs=(bst,)),
            ts_(la[:, :], x[:, :], st_[:, 1:2], OP.subtract,
                ins=(xb, bst), outs=(ba,))),
        lambda i, par, x, la, lb, lc, st_, xb, ba, bb_, bc_, bst:
            act(lb[:, :], la[:, :], AF.Square, ins=(ba,), outs=(bb_,)),
        lambda i, par, x, la, lb, lc, st_, xb, ba, bb_, bc_, bst: (
            red(st_[:, 2:3], lb[:, :], ins=(bb_,), outs=(bst,)),
            ts_(st_[:, 3:4], st_[:, 2:3], 1.0 / D, OP.mult,
                ins=(bst,), outs=(bst,))),
        lambda i, par, x, la, lb, lc, st_, xb, ba, bb_, bc_, bst: (
            act(st_[:, 4:5], st_[:, 3:4], AF.Ln, bias=EPS,
                ins=(bst,), outs=(bst,)),
            act(st_[:, 4:5], st_[:, 4:5], AF.Exp, scale=-0.5,
                ins=(bst,), outs=(bst,)),
            ts_(lb[:, :], la[:, :], st_[:, 4:5], OP.mult,
                ins=(ba, bst), outs=(bb_,))),
        lambda i, par, x, la, lb, lc, st_, xb, ba, bb_, bc_, bst: (
            copy(stat_rs[:, i:i + 1], st_[:, 4:5], ins=(bst,),
                 outs=(W(stat_rs),)),
            stt(stat_nmr[:, i:i + 1], st_[:, 1:2], -1.0, st_[:, 4:5],
                OP.mult, OP.mult, ins=(bst,), outs=(W(stat_nmr),)),
            stt(st_[:, 5:6], st_[:, 4:5], st_[:, 4:5], st_[:, 3:4],
                OP.mult, OP.mult, ins=(bst,), outs=(bst,))),
        lambda i, par, x, la, lb, lc, st_, xb, ba, bb_, bc_, bst: (
            act(st_[:, 6:7], st_[:, 5:6], AF.Ln, bias=EPS,
                ins=(bst,), outs=(bst,)),
            act(st_[:, 6:7], st_[:, 6:7], AF.Exp, scale=-0.5,
                ins=(bst,), outs=(bst,))),
        lambda i, par, x, la, lb, lc, st_, xb, ba, bb_, bc_, bst: (
            tt(st_[:, 7:8], st_[:, 4:5], st_[:, 6:7], OP.mult,
               ins=(bst,), outs=(bst,))),
        lambda i, par, x, la, lb, lc, st_, xb, ba, bb_, bc_, bst:
            ts_(lc[:, :], la[:, :], st_[:, 7:8], OP.mult,
                ins=(ba, bst), outs=(bc_,)),
        lambda i, par, x, la, lb, lc, st_, xb, ba, bb_, bc_, bst: (
            tt(lc[:, :], lc[:, :], gam_s[:, :], OP.mult,
               ins=(bc_, W(gam_s)), outs=(bc_,)),
            tt(lc[:, :], lc[:, :], bet_s[:, :], OP.add,
               ins=(bc_, W(bet_s)), outs=(bc_,))),
    ]
    for i in range(4):
        p0_stage(p0_dma, [i])
    for i in range(16):

        def p0_trn(i, par, x, la, lb, lc, st_, xb, ba, bb_, bc_, bst):
            nb = b_xnst[par]
            for half in range(2):
                for q in range(2):
                    bank = half * 2 + q
                    for j in range(2):
                        kc = half * 4 + q * 2 + j
                        trn(pb[bank][:, j * 128:(j + 1) * 128],
                            lb[:, kc * 128:(kc + 1) * 128],
                            cid_f[:, :], ins=(bb_, W(cid_f)), outs=(pB[bank],))
                    act(xnst[par].ap().rearrange("p (k t) -> p k t", k=KC)
                        [:, half * 4 + q * 2: half * 4 + q * 2 + 2, :],
                        pb[bank][:, 0:256].rearrange("p (k t) -> p k t", k=2),
                        AF.Copy, ins=(pB[bank],), outs=(nb,))
            for half in range(2):
                for q in range(2):
                    bank = 4 + half * 2 + q if half * 2 + q < 3 else 7
                    for j in range(2):
                        kc = half * 4 + q * 2 + j
                        trn(pb[bank][:, j * 128:(j + 1) * 128],
                            lc[:, kc * 128:(kc + 1) * 128],
                            cid_f[:, :], ins=(bc_, W(cid_f)), outs=(pB[bank],))
                    copy(xn2_bf.ap().rearrange("p (k t) -> p k t", k=KC)
                         [:, half * 4 + q * 2: half * 4 + q * 2 + 2,
                          i * 128:(i + 1) * 128],
                         pb[bank][:, 0:256].rearrange("p (k t) -> p k t", k=2),
                         ins=(pB[bank],), outs=(W(xn2_bf),))
            dma("sp",
                xn_h.rearrange("p (k t) -> p k t", k=KC)
                    [:, :, i * 128:(i + 1) * 128],
                xnst[par].ap().rearrange("p (k t) -> p k t", k=KC),
                ins=(nb,), outs=(b_xnh,))
        for si, st_fn in enumerate(p0_stages):
            p0_stage(st_fn, [i])
            if si == 0 and i + 4 < 16:
                p0_stage(p0_dma, [i + 4])
        p0_stage(p0_trn, [i])
        if i == 3:
            dma("pool", wproj_s[:, :], wproj_d, ins=(b_xt[3],),
                outs=(W(wproj_s),))
            dma("pool", wbc_s[:, :], wbc_d, outs=(W(wbc_s),))
            dma("pool", wdtf_s[:, :], wdtf_d, outs=(W(wdtf_s),))
        if i == 7:
            dma("pool", winf_s[:, :], winf_d, ins=(b_xt[3],),
                outs=(W(winf_s),))

    s.barrier()

    bA, bB, bC, bCS = W(sA), W(sB), W(sC), W(cs)
    dma("sp", xn2_h, xn2_bf[:, :], ins=(W(xn2_bf),), outs=(b_xn2h,))
    for gi in range(4):
        pk = 6
        for kc in range(KC):
            mm(pb[pk][0:32, 0:512],
               wdtf_s[:, kc * 32:(kc + 1) * 32],
               xn2_bf[:, kc * TOK + gi * 512: kc * TOK + (gi + 1) * 512],
               kc == 0, False, ins=(W(wdtf_s), W(xn2_bf)), outs=(pB[pk],))
        mm(pb[pk][0:32, 0:512], brow_dt[:, :], conesrs[:, :], False, False,
           ins=(W(brow_dt), W(conesrs)), outs=(pB[pk],))
        mm(pb[pk][0:32, 0:512], brow_dtb[:, :], conesrs[:, :], False, True,
           ins=(W(brow_dtb), W(conesrs)), outs=(pB[pk],))
        copy(sA[:, gi * 512:(gi + 1) * 512], pb[pk][0:8, 0:512],
             ins=(pB[pk],), outs=(bA,))
    for gi in range(4):
        for half, dst in ((0, B_fm), (1, C_fm)):
            pk = 6 + half
            for kc in range(KC):
                mm(pb[pk][0:ST, 0:512],
                   wbc_s[:, kc * 2 * ST + half * ST: kc * 2 * ST + (half + 1) * ST],
                   xn2_bf[:, kc * TOK + gi * 512: kc * TOK + (gi + 1) * 512],
                   kc == 0, False, ins=(W(wbc_s), W(xn2_bf)), outs=(pB[pk],))
            mm(pb[pk][0:ST, 0:512], brow_bc[:, half * ST:(half + 1) * ST],
               conesrs[:, :], False, True,
               ins=(W(brow_bc), W(conesrs)), outs=(pB[pk],))
            for cc in range(4):
                c = gi * 4 + cc
                act(dst[:, c * Q:(c + 1) * Q],
                    pb[pk][0:ST, cc * 128:(cc + 1) * 128],
                    AF.Copy, ins=(pB[pk],), outs=(W(dst),))
    act(sB[:, :], sA[:, :], AF.Abs, ins=(bA,), outs=(bB,))
    act(sB[:, :], sB[:, :], AF.Exp, scale=-1.0, ins=(bB,), outs=(bB,))
    act(sB[:, :], sB[:, :], AF.Ln, bias=1.0, ins=(bB,), outs=(bB,))
    ts_(sC[:, :], sA[:, :], 0.0, OP.max, ins=(bA,), outs=(bC,))
    tt(sA[:, :], sC[:, :], sB[:, :], OP.add, ins=(bC, bB), outs=(bA,))
    act(sB[:, :], sA[:, :], AF.Ln, ins=(bA,), outs=(bB,))
    ts_(sC[:, :], sA[:, :], acol[:, 0:1], OP.mult,
        ins=(bA, W(acol)), outs=(bC,))
    memset(cs[:, 0:1], 0.0, outs=(bCS,))
    s.op("dve",
         lambda e: e.tensor_tensor_scan(cs[:, 1:1 + TOK], sC[:, :],
                                        sC[:, :], 0.0, OP.add, OP.bypass),
         ins=(bC,), outs=(bCS,))
    for c in range(NCH):
        ts_(sC[:, c * Q:(c + 1) * Q], cs[:, 1 + c * Q:1 + (c + 1) * Q],
            cs[:, c * Q:c * Q + 1], OP.subtract, ins=(bCS,), outs=(bC,))
    copy(pk_hi[:, :], sC[:, :], ins=(bC,), outs=(W(pk_hi),))
    tt(pk_lo[:, :], sC[:, :], pk_hi[:, :], OP.subtract,
       ins=(bC, W(pk_hi)), outs=(W(pk_lo),))
    act(ecs_bf[:, :], sC[:, :], AF.Exp, ins=(bC,), outs=(W(ecs_bf),))
    tt(sC[:, :], sB[:, :], sC[:, :], OP.subtract, ins=(bB, bC), outs=(bC,))
    copy(pg_hi[:, :], sC[:, :], ins=(bC,), outs=(W(pg_hi),))
    tt(pg_lo[:, :], sC[:, :], pg_hi[:, :], OP.subtract,
       ins=(bC, W(pg_hi)), outs=(W(pg_lo),))
    ts_(sB[:, :], cs[:, 1:1 + TOK], cs[:, TOK:TOK + 1], OP.subtract,
        ins=(bCS,), outs=(bB,))
    act(sB[:, :], sB[:, :], AF.Exp, scale=-1.0, ins=(bB,), outs=(bB,))
    tt(sB[:, :], sB[:, :], sA[:, :], OP.mult, ins=(bB, bA), outs=(bB,))
    memset(hloc_sb[:, :], 0.0, outs=(W(hloc_sb),))

    def b_state(c):
        if c > 0:
            for h2 in range(4):
                tt(hloc_sb[:, h2 * 512:(h2 + 1) * 512],
                   hloc_sb[:, h2 * 512:(h2 + 1) * 512],
                   pb[2 + h2][0:ST, 0:512], OP.add,
                   ins=(W(hloc_sb), pB[2 + h2]), outs=(W(hloc_sb),))
        tt(bsc_all[c % 2].ap().rearrange("p (a b) -> p a b", a=8),
           B_tm.ap()[:, c * ST:(c + 1) * ST].unsqueeze(1)
               .broadcast_to([128, 8, ST]),
           bsc_tm.ap()[:, c * 8:(c + 1) * 8].unsqueeze(2)
               .broadcast_to([128, 8, ST]),
           OP.mult, ins=(W(B_tm), W(bsc_tm)), outs=(W(bsc_all[c % 2]),))
        for h in range(HEADS):
            bank = 2 + h // 2
            col = (h % 2) * 256
            mm(pb[bank][0:ST, col:col + 256],
               bsc_all[c % 2][:, h * ST:(h + 1) * ST],
               xsb[c % 3][:, h * HD:(h + 1) * HD],
               h % 2 == 0, h % 2 == 1,
               ins=(W(bsc_all[c % 2]), xb_list[c % 3]), outs=(pB[bank],),
               inc=(h % 2 == 1))

    xb_list = b_xsb
    for c in range(NCH):
        if c == 3:
            for c2 in range(NCH):
                pk7 = 6 + c2 % 2
                trn(pb[pk7][:, 128:136], sB[:, c2 * Q:(c2 + 1) * Q],
                    cid_f[0:8, 0:8], ins=(bB, W(cid_f)), outs=(pB[pk7],))
                copy(bsc_tm[:, c2 * 8:(c2 + 1) * 8], pb[pk7][:, 128:136],
                     ins=(pB[pk7],), outs=(W(bsc_tm),))
        if c >= 3:
            b_state(c - 3)
        xb = b_xsb[c % 3]
        for seg in range(5):
            w0 = seg * 512
            wid = 512 if seg < 4 else ST
            pk = seg % 2
            for kc in range(KC):
                mm(pb[pk][:, 0:wid],
                   xn2_bf[:, kc * TOK + c * Q: kc * TOK + (c + 1) * Q],
                   wproj_s[:, kc * PROJ_TM + w0: kc * PROJ_TM + w0 + wid],
                   kc == 0, False, ins=(W(xn2_bf), W(wproj_s)), outs=(pB[pk],))
            mm(pb[pk][:, 0:wid], conesrs[0:1, 0:128],
               brow_xsb[:, w0:w0 + wid], False, True,
               ins=(W(conesrs), W(brow_xsb)), outs=(pB[pk],))
            if seg < 4:
                act(xsb[c % 3][:, w0:w0 + 512], pb[pk][:, 0:512], AF.Copy,
                    ins=(pB[pk],), outs=(xb,))
            else:
                act(B_tm[:, c * ST:(c + 1) * ST], pb[pk][:, 0:ST], AF.Copy,
                    ins=(pB[pk],), outs=(W(B_tm),))
        dma("sp", xs_h[c * 128:(c + 1) * 128, :], xsb[c % 3][:, :],
            ins=(xb,), outs=(b_xsh[c],))
    tt(slog16[:, :], cs.ap()[:, 128::128], cs.ap()[:, 0:TOK:128], OP.subtract,
       ins=(bCS,), outs=(W(slog16),))
    memset(esc16[:, 16:32], 0.0, outs=(W(esc16),))
    act(esc16[:, 0:16], slog16[:, :], AF.Exp, ins=(W(slog16), W(esc16)),
        outs=(W(esc16),))
    trn(pb[7][0:32, 0:8], esc16[:, :], cid_f[0:8, 0:8],
        ins=(W(esc16), W(cid_f)), outs=(pB[7],))
    act(esc16b[:, :], pb[7][0:16, 0:8], AF.Copy, ins=(pB[7],), outs=(W(esc16b),))
    copy(tmq[0:16, 0:8], pb[7][0:16, 0:8], ins=(pB[7],), outs=(W(tmq),))
    dma("sp", es_h, esc16b[:, :], ins=(W(esc16b),), outs=(b_esh,))
    dma("sp", eSrow_hi[:, :], es_h.rearrange("c h -> () (c h)"),
        ins=(b_esh,), outs=(W(eSrow_hi),))
    tt(tmq[0:16, 0:8], tmq[0:16, 0:8], esc16b[:, :], OP.subtract,
       ins=(W(tmq), W(esc16b)), outs=(W(tmq),))
    copy(esc16b[:, :], tmq[0:16, 0:8], ins=(W(tmq),), outs=(W(esc16b),))
    dma("sp", es_h, esc16b[:, :], ins=(W(esc16b),), outs=(b_esh,))
    dma("sp", eSrow_lo[:, :], es_h.rearrange("c h -> () (c h)"),
        ins=(b_esh,), outs=(W(eSrow_lo),))
    for ct in range(NCH - 3, NCH):
        b_state(ct)
    for h2 in range(4):
        tt(hloc_sb[:, h2 * 512:(h2 + 1) * 512],
           hloc_sb[:, h2 * 512:(h2 + 1) * 512],
           pb[2 + h2][0:ST, 0:512], OP.add,
           ins=(W(hloc_sb), pB[2 + h2]), outs=(W(hloc_sb),))
    dma("sp", cc_in[0:ST, :], hloc_sb[:, :], ins=(W(hloc_sb),), outs=(b_ccin,))
    memset(sA[0:1, :], 0.0, ins=(bA,), outs=(bA,))
    dma("sp", cc_in[ST:ST + 1, :], sA[0:1, :], ins=(bA,), outs=(b_ccin,))
    dma("sp", cc_in[ST:ST + 1, 0:8].rearrange("o h -> h o"),
        cs[:, TOK:TOK + 1], ins=(bCS,), outs=(b_ccin,))

    ccw = s._wait_list("pool", (b_ccin,), (b_ccout,), False)
    s.cnt["cc"] += 1
    b_ccout.w = ("cc", s.cnt["cc"])

    def th_cc(engine, waits=tuple(ccw)):
        for sk, v in waits:
            engine.wait_ge(s.sems[sk], v)
        engine.collective_compute(
            "AllGather", OP.bypass,
            replica_groups=[list(range(ncores))],
            ins=[cc_in.opt()], outs=[cc_out.opt()],
        ).then_inc(s.sems["cc"], 1)
    s.thunks["pool"].append(th_cc)

    s.barrier()

    table("silu_and_others")
    memset(seltm[:, 0:8], 1.0, outs=(W(seltm),))
    memset(seltm[:, 120:128], 0.0, ins=(W(seltm),), outs=(W(seltm),))
    b_xnq = [Buf(f"xnq{qd}") for qd in range(4)]
    for qd in range(4):
        for k2 in range(4):
            dma("sp",
                xn_bf.ap().rearrange("p (k t) -> p k t", k=KC)
                     [:, k2 * 2:k2 * 2 + 2, qd * 512:(qd + 1) * 512],
                xn_h.rearrange("p (k t) -> p k t", k=KC)
                    [:, k2 * 2:k2 * 2 + 2, qd * 512:(qd + 1) * 512],
                ins=(b_xnh,), outs=(b_xnq[qd],))
    dma("pool", woutf_s[:, :], woutf_d, ins=(b_xnq[0],),
        outs=(W(woutf_s),))
    bsilL = [W(silzfA[i]) for i in range(2)]
    bdecL = [W(decA[i]) for i in range(2)]
    bselL = [W(selA[i]) for i in range(2)]

    def a_logits(gi):
        gp = gi % 2
        t0 = gi * 256
        bsil, bdec = bsilL[gp], bdecL[gp]
        for jc in range(16):
            pk = jc % 2
            for kc in range(KC):
                mm(pb[pk][:, 0:256],
                   winf_s[:, kc * NJ + jc * 128: kc * NJ + (jc + 1) * 128],
                   xn_bf[:, kc * TOK + t0: kc * TOK + t0 + 256],
                   kc == 0, False, ins=(W(winf_s), b_xnq[gi // 2]),
                   outs=(pB[pk],))
            mm(pb[pk][:, 0:256], brow_inf[:, jc * 128:(jc + 1) * 128],
               conesrs[0:1, 0:256], False, True,
               ins=(W(brow_inf), W(conesrs)), outs=(pB[pk],))
            act(silzfA[gp][:, jc * 256:(jc + 1) * 256], pb[pk][:, 0:256],
                AF.Silu, ins=(pB[pk],), outs=(bsil,))
            if 1 <= jc <= 7:
                ts_(decA[gp][:, (jc - 1) * 256:jc * 256],
                    silzfA[gp][:, jc * 256:(jc + 1) * 256],
                    0.0, OP.is_gt, ins=(bsil,), outs=(bdec,))
            if jc == 0:
                copy(l0f32[gp][:, :], pb[pk][:, 0:256],
                     ins=(pB[pk], bsil), outs=(W(l0f32[gp]),))

    def a_tree(gi):
        gp = gi % 2
        bsil, bdec, bsel = bsilL[gp], bdecL[gp], bselL[gp]
        for sti in range(2):
            trn(pb[6][:, 0:128], l0f32[gp][:, sti * 128:(sti + 1) * 128],
                cid_f[:, :], ins=(W(l0f32[gp]), W(cid_f)), outs=(pB[6],))
            ts_(dectm[:, 0:120], pb[6][:, 0:120], 0.0, OP.is_gt,
                ins=(pB[6],), outs=(W(dectm),))
            for (p0_, n, c0_) in ((0, 8, 8), (8, 16, 24), (24, 32, 56)):
                tt(seltm[:, c0_ + n: c0_ + 2 * n], seltm[:, p0_:p0_ + n],
                   dectm[:, p0_:p0_ + n], OP.mult,
                   ins=(W(seltm), W(dectm)), outs=(W(seltm),))
                tt(seltm[:, c0_:c0_ + n], seltm[:, p0_:p0_ + n],
                   seltm[:, c0_ + n:c0_ + 2 * n], OP.subtract,
                   ins=(W(seltm),), outs=(W(seltm),))
            tt(sel4tm[:, 64:128], seltm[:, 56:120], dectm[:, 56:120], OP.mult,
               ins=(W(seltm), W(dectm)), outs=(W(sel4tm),))
            tt(sel4tm[:, 0:64], seltm[:, 56:120], sel4tm[:, 64:128], OP.subtract,
               ins=(W(seltm), W(sel4tm)), outs=(W(sel4tm),))
            trn(pbf(6)[:, 0:128], seltm[:, :], cid_b[:, :],
                ins=(W(seltm), W(cid_b)), outs=(pB[6],))
            copy(selA[gp][:, sti * 128:(sti + 1) * 128], pbf(6)[:, 0:128],
                 ins=(pB[6],), outs=(bsel,))
            trn(pbf(7)[:, 0:128], sel4tm[:, :], cid_b[:, :],
                ins=(W(sel4tm), W(cid_b)), outs=(pB[7],))
            copy(selA[gp][:, 256 + sti * 128: 256 + (sti + 1) * 128],
                 pbf(7)[:, 0:128], ins=(pB[7],), outs=(bsel,))
        for (srcs, dst0) in (((1,), 2), ((2, 3), 4), ((4, 5, 6, 7), 8)):
            nsrc = len(srcs)
            for k, sc_ in enumerate(srcs):
                right, left = dst0 + nsrc + k, dst0 + k
                tt(selA[gp][:, right * 256:(right + 1) * 256],
                   selA[gp][:, sc_ * 256:(sc_ + 1) * 256],
                   decA[gp][:, (sc_ - 1) * 256:sc_ * 256], OP.mult,
                   ins=(bsel, bdec), outs=(bsel,))
                tt(selA[gp][:, left * 256:(left + 1) * 256],
                   selA[gp][:, sc_ * 256:(sc_ + 1) * 256],
                   selA[gp][:, right * 256:(right + 1) * 256], OP.subtract,
                   ins=(bsel,), outs=(bsel,))
        tt(silzfA[gp][:, :], silzfA[gp][:, :], selA[gp][:, :], OP.mult,
           ins=(bsil, bsel), outs=(bsil,))

    def a_wout(gi):
        gp = gi % 2
        bsil = bsilL[gp]
        for sti in range(2):
            ti = gi * 2 + sti
            pk0 = 2 + (ti % 2) * 2
            for jc in range(16):
                for sg in range(2):
                    mm(pb[pk0 + sg][:, 0:512],
                       silzfA[gp][:, jc * 256 + sti * 128:
                                   jc * 256 + (sti + 1) * 128],
                       woutf_s[:, jc * D + sg * 512: jc * D + (sg + 1) * 512],
                       jc == 0, jc == 15,
                       ins=(bsil, W(woutf_s)), outs=(pB[pk0 + sg],),
                       inc=(jc == 15 and sg == 1))
            xb = b_xta[ti % 2]
            dma("sp", xta[ti % 2][:, :], x_d[ti * 128:(ti + 1) * 128, :],
                ins=(xb,), outs=(xb,))
            ob = b_oa[ti % 2]
            act(oa[ti % 2][:, :], xta[ti % 2][:, :], AF.Copy,
                scale=stat_rs[:, ti:ti + 1], bias=stat_nmr[:, ti:ti + 1],
                ins=(xb, W(stat_rs), W(stat_nmr)), outs=(ob,))
            tt(oa[ti % 2][:, :], oa[ti % 2][:, :], xta[ti % 2][:, :], OP.add,
               ins=(ob, xb), outs=(ob,))
            for sg in range(2):
                tt(oa[ti % 2][:, sg * 512:(sg + 1) * 512],
                   oa[ti % 2][:, sg * 512:(sg + 1) * 512],
                   pb[pk0 + sg][:, 0:512], OP.add,
                   ins=(ob, pB[pk0 + sg]), outs=(ob,))
            dma("sp", part_h[ti * 128:(ti + 1) * 128, :], oa[ti % 2][:, :],
                ins=(ob,), outs=(b_part[ti],))

    for gi in range(8):
        if gi >= 1:
            a_tree(gi - 1)
        a_logits(gi)
        if gi >= 1:
            a_wout(gi - 1)
    a_tree(7)
    a_wout(7)
    dma("pool", brow_z[:, :], b_z_d, outs=(W(brow_z),))
    dma("pool", brow_o[:, :], b_o_d, outs=(W(brow_o),))

    s.barrier()

    dma("pool", woutp[0][:, :], wout_d[:, 0:8 * D], outs=(W(woutp[0]),))
    dma("pool", wzp[0][:, :],
        wz_d.rearrange("p (k j) -> p k j", k=KC)[:, :, 0:1024],
        outs=(W(wzp[0]),))
    table("natural_log_exp_and_others")
    dma("sp", sg8[:, :],
        cc_out.rearrange("(j r) t -> j r t", r=ST + 1)[:, ST, 0:8],
        ins=(b_ccout,), outs=(W(sg8),))
    act(sg8[:, :], sg8[:, :], AF.Exp, ins=(W(sg8),), outs=(W(sg8),))
    ts_(emod8[:, :], sg8[:, :], mcol_s[:, 0:1], OP.mult, minv_s[:, 0:1], OP.add,
        ins=(W(sg8), W(mcol_s), W(minv_s)), outs=(W(emod8),))
    dma("sp", em_h, emod8[:, :], ins=(W(emod8),), outs=(b_emh,))
    dma("pool", emrow_b[:, :], em_h.rearrange("j h -> () (j h)"),
        ins=(b_emh,), outs=(W(emrow_b),))
    mm(pb[7][0:ST, 0:8], cones8s[0:1, 0:ST], mrow_s[:, :], True, True,
       ins=(W(cones8s), W(mrow_s)), outs=(pB[7],))
    copy(mbc[:, :], pb[7][0:ST, 0:8], ins=(pB[7],), outs=(W(mbc),))
    memset(hglob[:, :], 0.0, outs=(W(hglob),))

    def x_hload(j):
        hb = b_hload[j % 2]
        for piece in range(4):
            dma("sp", hload[j % 2][:, piece * 512:(piece + 1) * 512],
                cc_out[j * (ST + 1): j * (ST + 1) + ST,
                       piece * 512:(piece + 1) * 512],
                ins=(hb, b_ccout), outs=(hb,))

    x_hload(0)
    dma("sp", xn2sg[0][:, :],
        xn2_h.rearrange("p (k t) -> p k t", k=KC)[:, :, 0:512],
        ins=(b_xn2h, b_xn2sg[0]), outs=(b_xn2sg[0],))

    def x_fold():
        for j in range(ncores - 1):
            hb = b_hload[j % 2]
            if j + 1 < ncores - 1:
                x_hload(j + 1)
            mm(pb[7][0:ST, 0:8], cones8s[0:1, 0:ST],
               emrow_b[:, j * 8:(j + 1) * 8],
               True, True, ins=(W(cones8s), W(emrow_b)), outs=(pB[7],))
            copy(embc[:, :], pb[7][0:ST, 0:8], ins=(pB[7],), outs=(W(embc),))
            ts_(mh_x[:, :], hload[j % 2][:, :], mbc[:, j:j + 1], OP.mult,
                ins=(hb, W(mbc)), outs=(W(mh_x),))
            for h in range(HEADS):
                stt(hglob[:, h * HD:(h + 1) * HD],
                    hglob[:, h * HD:(h + 1) * HD],
                    embc[:, h:h + 1], mh_x[:, h * HD:(h + 1) * HD],
                    OP.mult, OP.add,
                    ins=(W(hglob), W(embc), W(mh_x)), outs=(W(hglob),))
    bscC = W(bsc_tm)
    tt(scC[:, :], pg_hi[:, :], pg_lo[:, :], OP.add,
       ins=(W(pg_hi), W(pg_lo)), outs=(W(scC),))
    for c in range(NCH):
        ts_(scC[:, c * Q:(c + 1) * Q], scC[:, c * Q:(c + 1) * Q],
            slog16[:, c:c + 1], OP.add, ins=(W(scC), W(slog16)), outs=(W(scC),))
    act(scC[:, :], scC[:, :], AF.Exp, ins=(W(scC),), outs=(W(scC),))
    for c in range(NCH):
        pk = c % 2
        trn(pb[pk][:, 128:136], scC[:, c * Q:(c + 1) * Q], cid_f[0:8, 0:8],
            ins=(W(scC), W(cid_f)), outs=(pB[pk],))
        copy(bsc_tm[:, c * 8:(c + 1) * 8], pb[pk][:, 128:136],
             ins=(pB[pk],), outs=(bscC,))
    for c in range(NCH):
        mm(pb[2][0:ST, c * 8:(c + 1) * 8], cones8s[0:1, 0:ST],
           eSrow_hi[:, c * 8:(c + 1) * 8], True, False,
           ins=(W(cones8s), W(eSrow_hi)), outs=(pB[2],), inc=False)
        mm(pb[2][0:ST, c * 8:(c + 1) * 8], cones8s[0:1, 0:ST],
           eSrow_lo[:, c * 8:(c + 1) * 8], False, True,
           ins=(W(cones8s), W(eSrow_lo)), outs=(pB[2],))
    copy(eSbcA[:, :], pb[2][0:ST, 0:128], ins=(pB[2],), outs=(W(eSbcA),))
    for c in range(NCH):
        pk = c % 2
        mm(pb[pk][:, 0:128], B_fm[:, c * Q:(c + 1) * Q],
           C_fm[:, c * Q:(c + 1) * Q], True, True,
           ins=(W(B_fm), W(C_fm)), outs=(pB[pk],))
        copy(gbfA[:, c * 128:(c + 1) * 128], pb[pk][:, 0:128],
             ins=(pB[pk],), outs=(W(gbfA),))

    def c_eb(p, c):
        ci = c % 2
        cb = cblk0s if p == 0 else cblk1s
        bcb = W(cb)
        sl8 = slice(c * Q, (c + 1) * Q)
        tt(ebh[ci].ap().rearrange("p (a t) -> p a t", a=4),
           cb.ap().rearrange("p (a t) -> p a t", a=4),
           pk_hi.ap()[:, sl8].unsqueeze(1).broadcast_to([8, 4, 128]),
           OP.mult, ins=(bcb, W(pk_hi)), outs=(W(ebh[ci]),))
        tt(ebl[ci].ap().rearrange("p (a t) -> p a t", a=4),
           cb.ap().rearrange("p (a t) -> p a t", a=4),
           pk_lo.ap()[:, sl8].unsqueeze(1).broadcast_to([8, 4, 128]),
           OP.mult, ins=(bcb, W(pk_lo)), outs=(W(ebl[ci]),))
        tt(ebc[ci].ap().rearrange("p (a t) -> p a t", a=4),
           cb.ap().rearrange("p (a t) -> p a t", a=4),
           ecs_bf.ap()[:, sl8].unsqueeze(1).broadcast_to([8, 4, 128]),
           OP.mult, ins=(bcb, W(ecs_bf)), outs=(W(ebc[ci]),))

    def c_front(p, c):
        pc = p * 1024
        ci = c % 2
        cb = cblk0s if p == 0 else cblk1s
        bcb = W(cb)
        sl8 = slice(c * Q, (c + 1) * Q)
        xb = b_xsc[c % 3]
        dma("sp", xsc[c % 3][:, :],
            xs_h[c * 128:(c + 1) * 128, pc:pc + 1024],
            ins=(xb, b_xsh[c]), outs=(xb,))
        ptb = b_ptc[c % 2]
        dma("sp", ptc[c % 2][:, :], part_h[c * 128:(c + 1) * 128, :],
            ins=(ptb, b_part[c]), outs=(ptb,))
        mm(pb[ci][:, 0:512], cones8s[:, :], ebh[ci][:, :], True, False,
           ins=(W(cones8s), W(ebh[ci])), outs=(pB[ci],))
        mm(pb[ci][:, 0:512], cones8s[:, :], ebl[ci][:, :], False, False,
           ins=(W(cones8s), W(ebl[ci])), outs=(pB[ci],))
        mm(pb[ci][:, 0:512], pg_hi[:, sl8], cb[:, :], False, False,
           ins=(W(pg_hi), bcb), outs=(pB[ci],))
        mm(pb[ci][:, 0:512], pg_lo[:, sl8], cb[:, :], False, False,
           ins=(W(pg_lo), bcb), outs=(pB[ci],))
        mm(pb[ci][:, 0:512], cid_b[:, :], cmnegs[:, :], False, True,
           ins=(W(cid_b), W(cmnegs)), outs=(pB[ci],))
        act(ebE[ci][:, :], pb[ci][:, 0:512], AF.Exp,
            ins=(pB[ci],), outs=(W(ebE[ci]),))
        mm(pb[1 - ci][0:ST, 0:512], cones8s[:, 0:ST], ebc[ci][:, :],
           True, True, ins=(W(cones8s), W(ebc[ci])), outs=(pB[1 - ci],))
        tt(cecs[ci].ap().rearrange("p (a t) -> p a t", a=4),
           C_fm.ap()[:, sl8].unsqueeze(1).broadcast_to([ST, 4, 128]),
           pb[1 - ci][0:ST, 0:512].rearrange("p (a t) -> p a t", a=4),
           OP.mult, ins=(W(C_fm), pB[1 - ci]), outs=(W(cecs[ci]),))

    def c_back(p, c):
        pc = p * 1024
        h0g = p * 4
        ci = c % 2
        cc = c % 8
        tt(wch[ci].ap().rearrange("p (a t) -> p a t", a=4),
           ebE[ci].ap().rearrange("p (a t) -> p a t", a=4),
           gbfA.ap()[:, c * 128:(c + 1) * 128].unsqueeze(1)
               .broadcast_to([128, 4, 128]),
           OP.mult, ins=(W(ebE[ci]), W(gbfA)), outs=(W(wch[ci]),))
        tt(wch[ci][:, :], wch[ci][:, :], idsk[:, p * 512:(p + 1) * 512],
           OP.add, ins=(W(wch[ci]), W(idsk)), outs=(W(wch[ci]),))
        copy(hglob_bf[:, pc:pc + 1024], hglob[:, pc:pc + 1024],
             ins=(W(hglob),), outs=(W(hglob_bf),))
        if c + 2 <= 15:
            c_eb(p, c + 2)
        xb = b_xsc[c % 3]
        ptb = b_ptc[c % 2]
        for hh in range(4):
            for dc in range(2):
                idx = hh * 2 + dc
                bank = 2 + idx // 4
                col = (idx % 4) * 128
                mm(pb[bank][:, col:col + 128],
                   xsc[c % 3][:, hh * HD + dc * 128:
                              hh * HD + (dc + 1) * 128],
                   wch[ci][:, hh * 128:(hh + 1) * 128],
                   True, False, ins=(xb, W(wch[ci])), outs=(pB[bank],))
                mm(pb[bank][:, col:col + 128],
                   hglob_bf[:, (h0g + hh) * HD + dc * 128:
                            (h0g + hh) * HD + (dc + 1) * 128],
                   cecs[ci][:, hh * 128:(hh + 1) * 128],
                   False, True, ins=(W(hglob_bf), W(cecs[ci])),
                   outs=(pB[bank],))
        yb = b_ygb[c % 2]
        for bk in range(2):
            tt(ygb[c % 2][:, bk * 512:(bk + 1) * 512],
               pb[2 + bk][:, 0:512],
               silz8[:, cc * 1024 + bk * 512: cc * 1024 + (bk + 1) * 512],
               OP.mult, ins=(pB[2 + bk], W(silz8)), outs=(yb,))
        tt(bsc2[ci].ap().rearrange("p (a b) -> p a b", a=4),
           B_tm.ap()[:, c * ST:(c + 1) * ST].unsqueeze(1)
               .broadcast_to([128, 4, ST]),
           bsc_tm.ap()[:, c * 8 + h0g: c * 8 + h0g + 4].unsqueeze(2)
               .broadcast_to([128, 4, ST]),
           OP.mult, ins=(W(B_tm), bscC), outs=(W(bsc2[ci]),))
        for hh in range(4):
            bank = 4 + hh // 2
            col = (hh % 2) * 256
            mm(pb[bank][0:ST, col:col + 256],
               bsc2[ci][:, hh * ST:(hh + 1) * ST],
               xsc[c % 3][:, hh * HD:(hh + 1) * HD],
               True, True, ins=(W(bsc2[ci]), xb), outs=(pB[bank],))
        for hh in range(4):
            h = h0g + hh
            bank = 4 + hh // 2
            col = (hh % 2) * 256
            stt(hglob[:, h * HD:(h + 1) * HD],
                hglob[:, h * HD:(h + 1) * HD],
                eSbcA[:, c * 8 + h:c * 8 + h + 1],
                pb[bank][0:ST, col:col + 256], OP.mult, OP.add,
                ins=(W(hglob), W(eSbcA), pB[bank]), outs=(W(hglob),))
        for jc in range(8):
            for sgm in range(2):
                mm(pb[6 + sgm][:, 0:512],
                   ygb[c % 2][:, jc * 128:(jc + 1) * 128],
                   woutp[p][:, jc * D + sgm * 512: jc * D + (sgm + 1) * 512],
                   jc == 0, jc == 7 and p == 1,
                   ins=(yb, W(woutp[p])), outs=(pB[6 + sgm],),
                   inc=(jc == 7 and p == 1))
        if p == 0:
            mm(pb[6][:, 0:512], conesrs[0:1, 0:128], brow_o[:, 0:512],
               False, True, ins=(W(conesrs), W(brow_o)), outs=(pB[6],))
            mm(pb[7][:, 0:512], conesrs[0:1, 0:128], brow_o[:, 512:1024],
               False, True, ins=(W(conesrs), W(brow_o)), outs=(pB[7],))
        for sgm in range(2):
            tt(ptc[c % 2][:, sgm * 512:(sgm + 1) * 512],
               ptc[c % 2][:, sgm * 512:(sgm + 1) * 512],
               pb[6 + sgm][:, 0:512], OP.add,
               ins=(ptb, pB[6 + sgm]), outs=(ptb,))
        if p == 0:
            dma("sp", part_h[c * 128:(c + 1) * 128, :], ptc[c % 2][:, :],
                ins=(ptb,), outs=(b_part[c],))
        else:
            dma("sp", y_d[c * 128:(c + 1) * 128, :], ptc[c % 2][:, :],
                ins=(ptb,), outs=(b_y,))

    def c_zproj(p, h8):
        pc = p * 1024
        cbase = h8 * 8
        table("silu_and_others")
        for sgi in range(2):
            sgb = b_xn2sg[(cbase // 4 + sgi) % 2]
            for jc in range(8):
                pk = jc % 2
                for kc in range(KC):
                    mm(pb[pk][:, 0:512],
                       wzp[p][:, kc * 1024 + jc * 128:
                              kc * 1024 + (jc + 1) * 128],
                       xn2sg[(cbase // 4 + sgi) % 2][:, kc * 512:(kc + 1) * 512],
                       kc == 0, False, ins=(W(wzp[p]), sgb), outs=(pB[pk],))
                mm(pb[pk][:, 0:512],
                   brow_z[:, pc + jc * 128: pc + (jc + 1) * 128],
                   conesrs[:, :], False, True,
                   ins=(W(brow_z), W(conesrs)), outs=(pB[pk],))
                act(silz8.ap().rearrange("p (a b t) -> p a b t", a=8, b=8)
                    [:, sgi * 4:(sgi + 1) * 4, jc, :],
                    pb[pk][:, 0:512].rearrange("p (a t) -> p a t", a=4),
                    AF.Silu, ins=(pB[pk],), outs=(W(silz8),))
            nxt = cbase // 4 + sgi + 1
            if nxt < 4 or (p == 0 and nxt == 4):
                t1 = (nxt % 4) * 512
                nb = b_xn2sg[nxt % 2]
                dma("sp", xn2sg[nxt % 2][:, :],
                    xn2_h.rearrange("p (k t) -> p k t", k=KC)
                         [:, :, t1:t1 + 512],
                    ins=(b_xn2h, nb), outs=(nb,))
        table("natural_log_exp_and_others")

    c_zproj(0, 0)
    x_fold()

    s.barrier()

    prev = None
    for p in range(2):
        pc = p * 1024
        if p == 0:
            dma("pool", wzp[1][:, :],
                wz_d.rearrange("p (k j) -> p k j", k=KC)[:, :, 1024:2048],
                outs=(W(wzp[1]),))
            dma("pool", woutp[1][:, :], wout_d[:, 8 * D:16 * D],
                outs=(W(woutp[1]),))
        c_eb(p, 0)
        c_eb(p, 1)
        for h8 in range(2):
            cbase = h8 * 8
            if prev is not None:
                c_back(*prev)
                prev = None
            if not (p == 0 and h8 == 0):
                c_zproj(p, h8)
            for cc in range(8):
                c = cbase + cc
                c_front(p, c)
                if prev is not None:
                    c_back(*prev)
                prev = (p, c)
    c_back(*prev)

    finals = [(k, s.cnt[k]) for k in DMAKEYS if k.startswith("dma_sp") and s.cnt[k]]

    def th_fin(engine):
        for k, v in finals:
            engine.wait_ge(s.sems[k], v)
    s.thunks["sp"].append(th_fin)

    from contextlib import ExitStack
    with ExitStack() as stack:
        s.sems = {}
        for k in SEMKEYS:
            s.sems[k] = stack.enter_context(nc.semaphore("s_" + k))
        block = stack.enter_context(nc.Block())

        @block.tensor
        def _(eng):
            for th in s.thunks["pe"]:
                th(eng)

        @block.scalar
        def _(eng):
            for th in s.thunks["act"]:
                th(eng)

        @block.vector
        def _(eng):
            for th in s.thunks["dve"]:
                th(eng)

        @block.gpsimd
        def _(eng):
            for th in s.thunks["pool"]:
                th(eng)

        @block.sync
        def _(eng):
            for th in s.thunks["sp"]:
                th(eng)

    return nc


def kernel(**inputs):
    maps = prep_inputs(inputs)
    nc = build(NCORES)
    res = run_bass_kernel_spmd(nc, maps, core_ids=list(range(NCORES)))
    out = np.concatenate([np.asarray(r["y"]) for r in res.results], axis=0)
    return out.reshape(1, 16384, D).astype(np.float32)
